# version 15
# baseline (speedup 1.0000x reference)
import math
import numpy as np
import ml_dtypes
import concourse.bass as bass
import concourse.mybir as mybir
from concourse.bass_utils import run_bass_kernel_spmd

F32 = mybir.dt.float32
BF16 = mybir.dt.bfloat16
AF = mybir.ActivationFunctionType
ALU = mybir.AluOpType
AX = mybir.AxisListType

D = 2048
DC = 16
I_DIM = 5632
IC = 44
S = 2048
B_CORE = 2
TG = 1024
NG = 4
NU = 2
WIN = 256
NH = 16
HD = 128
NEG = -1e30
N_CORES = 8

STAGE = 3


class Ctr:
    __slots__ = ("h", "n", "name")
    def __init__(self, h, name):
        self.h = h
        self.n = 0
        self.name = name

class Prog:
    ENGS = ("tensor", "vector", "scalar", "sync", "gpsimd")

    def __init__(self):
        self.ops = {e: [] for e in self.ENGS}
        self.pending = {e: [] for e in self.ENGS}
        self.seen = {e: {} for e in self.ENGS}
        self.since_barrier = {e: None for e in self.ENGS}

    def emit(self, eng, fn, waits=(), inc=None, inc_n=1):
        wl = self.pending[eng]
        self.pending[eng] = []
        for (c, v) in waits:
            wl.append((c, v))
        out_waits = []
        for (c, v) in wl:
            prev = self.seen[eng].get(c.h.num, -1)
            if v > prev:
                self.seen[eng][c.h.num] = v
                out_waits.append((c.h, v))
        incs = []
        ret = None
        if inc is not None:
            inc.n += inc_n
            incs.append((inc.h, inc_n))
            ret = inc.n
        self.ops[eng].append([fn, out_waits, incs])
        self.since_barrier[eng] = len(self.ops[eng]) - 1
        return ret

    def ensure_inc(self, eng, ctr):
        idx = self.since_barrier[eng]
        if idx is None:
            return
        rec = self.ops[eng][idx]
        for (h, n) in rec[2]:
            if h.num == ctr.h.num:
                return
        ctr.n += 1
        rec[2].append((ctr.h, 1))

    def barrier(self, eng_ctrs, all_ctrs):
        snap = [(c, c.n) for c in all_ctrs if c.n > 0]
        for e in self.ENGS:
            own = eng_ctrs.get(e, ())
            own_nums = {c.h.num for c in own}
            for (c, v) in snap:
                if c.h.num not in own_nums:
                    self.pending[e].append((c, v))
            self.since_barrier[e] = None


def build_block(nc, prog):
    with nc.Block() as block:
        for eng_name in Prog.ENGS:
            ops = prog.ops[eng_name]
            def runner(eng, ops=ops):
                for fn, waits, incs in ops:
                    for (h, v) in waits:
                        eng.wait_ge(h, v)
                    ins = fn(eng) if fn is not None else None
                    if incs:
                        assert ins is not None
                        for (h, n) in incs:
                            ins.then_inc(h, n)
            getattr(block, eng_name)(runner)



def make_tables():
    inv_freq = (10000.0 ** (-(np.arange(0, 64, dtype=np.float32)) / 64.0)).astype(np.float32)
    t = np.arange(S, dtype=np.float32)
    ang = (t[:, None] * inv_freq[None, :]).astype(np.float32)
    cos = np.cos(ang).astype(np.float32).T
    sin = np.sin(ang).astype(np.float32).T
    sc = np.float32(1.0 / math.sqrt(HD))
    cosf = np.concatenate([cos, cos], axis=0)
    sinf = np.concatenate([sin, sin], axis=0)
    tbl = np.stack([cosf, sinf, cosf * sc, sinf * sc], axis=1)
    j = np.arange(256)[:, None]
    q = np.arange(256)[None, :]
    mask23 = np.where(j <= q, 0.0, NEG).astype(np.float32)
    m = np.zeros((128, 2, 256), dtype=np.float32)
    m[:, 0, :] = mask23[0:128, :]
    m[:, 1, :] = mask23[128:256, :]
    rotm = np.zeros((128, 128), dtype=ml_dtypes.bfloat16)
    for d in range(64):
        rotm[d + 64, d] = -1.0
    for d in range(64, 128):
        rotm[d - 64, d] = 1.0
    return np.ascontiguousarray(tbl), np.ascontiguousarray(m), rotm



ARENA = 163840
WB = 24576

class MK:
    def __init__(self, nc, stage=STAGE):
        self.nc = nc
        self.stage = stage
        self.p = Prog()

        dt = nc.dram_tensor
        self.xin = dt("xin", [B_CORE, D, S], F32, kind="ExternalInput").ap()
        self.gw = dt("gw", [IC, 128, DC * 128], F32, kind="ExternalInput").ap()
        self.uw = dt("uw", [IC, 128, DC * 128], F32, kind="ExternalInput").ap()
        self.dw = dt("dw", [DC, 128, IC * 128], F32, kind="ExternalInput").ap()
        self.gsd = dt("gsd", [128, IC], F32, kind="ExternalInput").ap()
        self.usd = dt("usd", [128, IC], F32, kind="ExternalInput").ap()
        self.dsd = dt("dsd", [128, DC], F32, kind="ExternalInput").ap()
        self.wq = dt("wqt", [DC, 128, DC * 128], F32, kind="ExternalInput").ap()
        self.wk = dt("wkt", [DC, 128, DC * 128], F32, kind="ExternalInput").ap()
        self.wv = dt("wvt", [DC, 128, DC * 128], F32, kind="ExternalInput").ap()
        self.wo = dt("wot", [DC, 128, DC * 128], F32, kind="ExternalInput").ap()
        self.y = dt("y", [B_CORE, D, S], F32, kind="ExternalOutput").ap()
        self.x1 = dt("x1", [B_CORE, D, S], F32).ap()
        self.x2 = dt("x2", [B_CORE, D, S], F32).ap()
        self.kprev_d = dt("kprev_d", [128, NH, WIN], BF16).ap()
        self.vprev_d = dt("vprev_d", [128, 2, D], BF16).ap()

        tbl, mask23, rotm = make_tables()
        it = nc.inline_tensor
        self.c_tbl = it(tbl, name="c_tbl").ap()
        self.c_mask23 = it(mask23, name="c_mask23").ap()
        self.c_rotm = it(rotm, name="c_rotm").ap()
        self.c_id128f = it(np.eye(128, dtype=np.float32), name="c_id128f").ap()
        self.c_ident = it(np.eye(128, dtype=ml_dtypes.bfloat16), name="c_ident").ap()
        self.c_ones = it(np.ones((128, 128), dtype=ml_dtypes.bfloat16), name="c_ones").ap()
        self.c_negones = it(np.full((1, 128), -1.0, dtype=ml_dtypes.bfloat16), name="c_negones").ap()
        self.c_invd = it(np.full((128, 128), 1.0 / D, dtype=ml_dtypes.bfloat16), name="c_invd").ap()
        self.c_ones16 = it(np.ones((16, 128), dtype=ml_dtypes.bfloat16), name="c_ones16").ap()
        self.c_id16 = it(np.eye(16, dtype=np.float32), name="c_id16").ap()
        self.c_eps = it(np.full((128, 1), 1e-5, dtype=np.float32), name="c_eps").ap()
        self.c_epsden = it(np.full((128, 1), 1e-20, dtype=np.float32), name="c_epsden").ap()

        a = nc.alloc_sbuf_tensor
        self.ident_sb = a("ident_sb", [128, 128], BF16)
        self.ones_sb = a("ones_sb", [128, 128], BF16)
        self.negones_sb = a("negones_sb", [1, 128], BF16)
        self.invd_sb = a("invd_sb", [128, 128], BF16)
        self.ones16_sb = a("ones16_sb", [16, 128], BF16)
        self.id16_sb = a("id16_sb", [16, 16], F32)
        self.eps_sb = a("eps_sb", [128, 1], F32)
        self.epsden_sb = a("epsden_sb", [128, 1], F32)
        self.mask23_sb = a("mask23_sb", [128, 2, WIN], F32)
        self.rotm_sb = a("rotm_sb", [128, 128], BF16)
        self.id128f_sb = a("id128f_sb", [128, 128], F32)
        self.qtmp_sb = a("qtmp_sb", [128, 4, 512], BF16)
        self.rp_a = a("rp_a", [128, 512], F32)
        self.rp_b = a("rp_b", [128, 512], F32)
        self.mtc_sb = a("mtc_sb", [128, 4], BF16)
        self.mth_sb = a("mth_sb", [128, 4, 2], F32)
        self.nmb_sb = a("nmb_sb", [1, 4], BF16)
        self.gs_sb = a("gs_sb", [128, IC], F32)
        self.us_sb = a("us_sb", [128, IC], F32)
        self.dsraw_sb = a("dsraw_sb", [128, DC], F32)
        self.dsh_sb = a("dsh_sb", [128, DC], F32)
        self.mt_sb = a("mt_sb", [128, 16], F32)
        self.row_sb = a("row_sb", [16, 128], F32)
        self.diag_sb = a("diag_sb", [16, 16], BF16)
        self.negb_sb = a("negb_sb", [128, 16], F32)
        self.scr64a = a("scr64a", [64, 512], F32)
        self.scr64b = a("scr64b", [64, 512], F32)
        self.vfm_sb = a("vfm_sb", [128, 2, 512], BF16)

        self.arena_h = a("arena", [128, ARENA], mybir.dt.uint8)
        self.wb_h = a("wbregion", [128, WB], mybir.dt.uint8)
        self.arena_base = nc.lookup_mloc(self.arena_h).addr
        self.wb_base = nc.lookup_mloc(self.wb_h).addr

        pl = self.place
        ab = self.arena_base
        wb = self.wb_base
        self.ln = pl("ln_sl", [128, DC, TG], BF16, ab + 0)
        self.h = pl("h_sl", [128, IC, TG], BF16, ab + 32768)
        self.xin_fd = pl("xin_fd", [128, 4, 512], F32, ab + 145408)
        self.xout_fd = pl("xout_fd", [128, 4, 512], F32, ab + 153600)
        self.sqbf_l1 = pl("sqbf_l1", [128, DC, 512], BF16, ab + 122880)
        self.sqbf = pl("sqbf_sl", [128, 4, 512], BF16, ab + 139776)
        self.mn_scr = pl("mn_scr", [128, NU, 512], F32, ab + 49152)
        self.var_scr = pl("var_scr", [128, NU, 512], F32, ab + 53248)
        self.sd_scr = pl("sd_scr", [128, NU, 512], F32, ab + 57344)
        self.r_scr = pl("r_scr", [128, NU, 512], BF16, ab + 61440)
        self.mean_sb = pl("mean_sb", [128, NU, 512], BF16, ab + 63488)
        self.d_scr = pl("d_scr", [128, 4, 512], BF16, ab + 65536)
        self.sg_scr = pl("sg_scr", [128, 2, 512], F32, ab + 145408)
        self.ut_scr = pl("ut_scr", [128, 2, 512], F32, ab + 149504)
        self.qrot = pl("qrot_sl", [128, NH, TG], BF16, ab + 32768)
        self.kbuf = pl("kbuf_sl", [128, NH, WIN + TG], BF16, ab + 65536)
        self.vbuf = pl("vbuf_sl", [128, 10, D], BF16, ab + 106496)
        self.tbl_sb = pl("tbl_sl", [128, 4, TG], F32, ab + 147456)
        self.scores = pl("scores_sl", [128, 4, 1024], F32, ab + 0)
        self.psb = pl("p_sl", [128, 4, 1024], BF16, ab + 16384)
        self.rc = pl("rc_sl", [128, 4, WIN], F32, ab + 24576)
        self.rcscr = pl("rcscr_sl", [128, 4, WIN], F32, ab + 28672)
        self.ao_sb = self.qrot
        self.xin_fo = pl("xin_fo", [128, 4, 512], F32, ab + 65536)
        self.xout_fo = pl("xout_fo", [128, 4, 512], F32, ab + 73728)
        self.abuf = pl("abuf", [128, 6, DC, 128], BF16, wb + 0)
        self.dbuf_h = pl("dbuf_h", [128, 3, 22, 128], BF16, ab + 122880)

        ap_ = nc.alloc_psum_tensor
        self.pb = [ap_(f"pb{i}", [128, 512], F32) for i in range(6)]
        self.pb_pl = ap_("pb_pl", [128, 512], F32)
        self.pb_t = ap_("pb_t", [128, 1024], BF16)

        sem = nc.alloc_semaphore
        self.pe = Ctr(sem(name="s_pe"), "pe")
        self.dve = Ctr(sem(name="s_dve"), "dve")
        self.act = Ctr(sem(name="s_act"), "act")
        self.gd = Ctr(sem(name="s_gd"), "gd")
        self.xd = Ctr(sem(name="s_xd"), "xd")
        self.sd = Ctr(sem(name="s_sd"), "sd")
        self.sx = [Ctr(sem(name=f"s_sx{i}"), f"sx{i}") for i in range(2)]
        self.sa = [Ctr(sem(name=f"s_sa{i}"), f"sa{i}") for i in range(6)]
        self.sb = [Ctr(sem(name=f"s_sb{i}"), f"sb{i}") for i in range(3)]
        self.sxi = [Ctr(sem(name=f"s_sxi{i}"), f"sxi{i}") for i in range(4)]
        self.sso = [Ctr(sem(name=f"s_sso{i}"), f"sso{i}") for i in range(4)]
        self.smisc = [Ctr(sem(name=f"s_sm{i}"), f"sm{i}") for i in range(2)]
        self.gp = Ctr(sem(name="s_gp"), "gp")
        self.all_ctrs = ([self.pe, self.dve, self.act, self.gd, self.xd, self.sd,
                          self.gp]
                         + self.sx + self.sa + self.sb + self.sxi + self.sso
                         + self.smisc)
        self.eng_ctrs = {"tensor": (self.pe,), "vector": (self.dve,),
                         "scalar": (self.act,), "sync": (), "gpsimd": (self.gp,)}
        self.sqbf_war = {}
        self.sh_done = {}
        self._shadow_q = []
        self.a_ctr = 0
        self.a_war = {}
        self.b_ctr = 0
        self.b_war = {}
        self.b_extra_w = []
        self.lnrdy = {}
        self.ln_bankfree = {}

    def place(self, name, shape, dtype, addr):
        nc = self.nc
        h = nc.virtual_sbuf_tensor(name, shape, dtype)
        per_part = 1
        for s_ in shape[1:]:
            per_part *= s_
        per_part *= mybir.dt.size(dtype)
        nc._tensor(h.name, list(shape), dtype, type="SB")
        mloc = nc.lookup_mloc(h)
        mloc.allocated = True
        mloc.addr = addr
        return h

    def bar(self):
        self.p.barrier(self.eng_ctrs, self.all_ctrs)

    def soften(self, **eng_waits):
        for e, wl in eng_waits.items():
            self.p.pending[e] = list(wl)
            self.p.seen[e] = {}

    def _sso_totals(self):
        return [(c, c.n) for c in self.sso if c.n > 0]


    def prep(self):
        p = self.p
        loads = [
            (self.ident_sb, self.c_ident), (self.ones_sb, self.c_ones),
            (self.negones_sb, self.c_negones),
            (self.invd_sb, self.c_invd), (self.ones16_sb, self.c_ones16),
            (self.id16_sb, self.c_id16), (self.eps_sb, self.c_eps),
            (self.epsden_sb, self.c_epsden),
            (self.mask23_sb, self.c_mask23), (self.rotm_sb, self.c_rotm),
            (self.id128f_sb, self.c_id128f),
            (self.gs_sb, self.gsd), (self.us_sb, self.usd),
            (self.dsraw_sb, self.dsd),
        ]
        for dst, src in loads:
            p.emit("sync", lambda e, d=dst, s=src: e.dma_start(d[tuple(slice(None) for _ in d.shape)], s[tuple(slice(None) for _ in s.shape)]),
                   inc=self.xd, inc_n=16)
        v_xd = self.xd.n
        p.emit("vector", lambda e: e.tensor_scalar_mul(self.dsh_sb[:, :], self.dsraw_sb[:, :], 0.5),
               waits=[(self.xd, v_xd)], inc=self.dve)
        self.bar()

    def lnin(self, g, src=None):
        p = self.p
        if not hasattr(self, "sqbf_war"):
            self.sqbf_war = {}
        b, toff = g // 2, (g % 2) * TG
        for u in range(NU):
            if src is not None:
                pbm_key, pbq_key = ("pb", 2 * u), ("pb", 2 * u + 1)
                pbm = self.pb[2 * (u % 2)]
                pbq = self.pb[2 * (u % 2) + 1]
                sq_marks = []
                msq_marks = []
                mean_mark = None
                sl = slice(toff + u * 512, toff + u * 512 + 512)
                mld = p.emit("gpsimd", lambda e, sl=sl, u=u: e.dma_start(
                    self.ln[:, :, u * 512:(u + 1) * 512],
                    src[b, :, sl].rearrange("(c p) t -> p c t", p=128)),
                    inc=self.sx[u], inc_n=16)
                if not hasattr(self, "sqbf_l1_war"):
                    self.sqbf_l1_war = {}
                for c in range(DC):
                    sq_w = [(self.sx[u], mld)]
                    if c in self.sqbf_l1_war:
                        sq_w.append(self.sqbf_l1_war[c])
                    m = p.emit("scalar", lambda e, c=c, u=u: e.activation(
                        self.sqbf_l1[:, c, :], self.ln[:, c, u * 512:(u + 1) * 512],
                        AF.Square),
                        waits=sq_w, inc=self.act)
                    sq_marks.append(m)
                for c in range(DC):
                    mean_mark = p.emit("tensor", lambda e, c=c, u=u, pbm=pbm: e.matmul(
                        pbm[:, :], self.invd_sb[:, :],
                        self.ln[:, c, u * 512:(u + 1) * 512],
                        start=(c == 0), stop=(c == DC - 1)),
                        waits=[(self.sx[u], mld)],
                        inc=self.pe)
                for c in range(DC):
                    mm = p.emit("tensor", lambda e, c=c, u=u, pbq=pbq: e.matmul(
                        pbq[:, :], self.invd_sb[:, :], self.sqbf_l1[:, c, :],
                        start=(c == 0), stop=(c == DC - 1)),
                        waits=[(self.act, sq_marks[c])],
                        inc=self.pe)
                    msq_marks.append(mm)
                    self.sqbf_l1_war[c] = (self.pe, mm)
                w_mn = [(self.pe, mean_mark)]
                w_vr = [(self.pe, msq_marks[-1])]
            else:
                pbm_key, pbq_key = (("pb", 3), ("pb", 4)) if u == 0 else (("pb", 5), ("pl",))
                pbm = self.pb[3] if u == 0 else self.pb[5]
                pbq = self.pb[4] if u == 0 else self.pb_pl
                w_mn = []
                w_vr = []
            m_act = p.emit("scalar", lambda e, u=u, pbm=pbm: e.activation(
                self.mn_scr[:, u, :], pbm[:, :], AF.Square),
                waits=w_mn, inc=self.act)
            m_mev = p.emit("scalar", lambda e, u=u, pbm=pbm: e.activation(
                self.mean_sb[:, u, :], pbm[:, :], AF.Copy),
                inc=self.act)
            self.ln_bankfree[pbm_key] = (self.act, m_mev)
            m_dve = p.emit("vector", lambda e, u=u, pbq=pbq: e.tensor_tensor(
                self.var_scr[:, u, :], pbq[:, :], self.mn_scr[:, u, :], ALU.subtract),
                waits=w_vr + [(self.act, m_act)], inc=self.dve)
            self.ln_bankfree[pbq_key] = (self.dve, m_dve)
            m_act = p.emit("scalar", lambda e, u=u: e.activation(
                self.sd_scr[:, u, :], self.var_scr[:, u, :], AF.Ln, bias=self.eps_sb[:, :]),
                waits=[(self.dve, m_dve)], inc=self.act)
            m_act = p.emit("scalar", lambda e, u=u: e.activation(
                self.r_scr[:, u, :], self.sd_scr[:, u, :], AF.Exp, scale=-0.5),
                waits=[(self.act, m_act)], inc=self.act)
            sub_marks = {}
            mul_marks = {}

            def emit_mul(c, u=u):
                w = [(self.dve, sub_marks[c])]
                if c == 0:
                    w.append((self.act, m_act))
                ml = p.emit("vector", lambda e, c=c, u=u: e.tensor_tensor(
                    self.ln[:, c, u * 512:(u + 1) * 512], self.d_scr[:, c % 4, :],
                    self.r_scr[:, u, :], ALU.mult),
                    waits=w, inc=self.dve)
                mul_marks[c] = ml
                self.lnrdy[(u, c)] = (self.dve, ml)

            for c in range(DC):
                w = [(self.act, m_act)] if c == 0 else []
                if c >= 4:
                    w.append((self.dve, mul_marks[c - 4]))
                sub_marks[c] = p.emit("vector", lambda e, c=c, u=u: e.tensor_tensor(
                    self.d_scr[:, c % 4, :], self.ln[:, c, u * 512:(u + 1) * 512],
                    self.mean_sb[:, u, :], ALU.subtract),
                    waits=w, inc=self.dve)
                if c >= 1:
                    emit_mul(c - 1)
            emit_mul(DC - 1)
        self.bar()

    def _aload(self, src_ap, extra_w=()):
        p = self.p
        slot = self.a_ctr % 6
        self.a_ctr += 1
        w = list(extra_w)
        if slot in self.a_war:
            w.append(self.a_war.pop(slot))
        m = p.emit("gpsimd", lambda e, slot=slot, src_ap=src_ap: e.dma_start(
            self.abuf[:, slot, :, :],
            src_ap.rearrange("p (c i) -> p c i", c=DC)),
            waits=w, inc=self.sa[slot], inc_n=16)
        return slot, m

    def fgu(self, ln_entry=None):
        p = self.p
        self.hmul_done = {}
        ev_done = {}
        pend_h = []

        def flush_h(lag=0):
            while len(pend_h) > lag:
                j_, u_, ssl_, ma_, md_ = pend_h.pop(0)
                mh = p.emit("vector", lambda e, j=j_, u=u_, ssl=ssl_: e.tensor_tensor(
                    self.h[:, j, u * 512:(u + 1) * 512], self.sg_scr[:, ssl, :],
                    self.ut_scr[:, ssl, :], ALU.mult),
                    waits=[(self.act, ma_), (self.dve, md_)], inc=self.dve)
                self.hmul_done[ssl_] = mh
        for j in range(IC):
            slg, mg = self._aload(self.gw[j, :, :])
            slu, mu = self._aload(self.uw[j, :, :])
            for u in range(NU):
                par = (j * NU + u) % 2
                pbg = self.pb[par]
                pbu = self.pb[2 + par]
                w_g = [(self.sa[slg], mg)]
                w_u = [(self.sa[slu], mu)]
                if j == 0 and ln_entry is not None:
                    for (bk, wl) in ((("pb", par), w_g), (("pb", 2 + par), w_u)):
                        if bk in self.ln_bankfree:
                            wl.append(self.ln_bankfree[bk])
                        wl.extend(ln_entry)
                key_g = ("g", par)
                key_u = ("u", par)
                if key_g in ev_done:
                    w_g.append((self.act, ev_done[key_g]))
                if key_u in ev_done:
                    w_u.append((self.dve, ev_done[key_u]))
                for c in range(DC):
                    cw = list(w_g) if c == 0 else []
                    if j == 0 and ln_entry is not None:
                        cw.append(self.lnrdy[(u, c)])
                    p.emit("tensor", lambda e, slg=slg, u=u, c=c, pbg=pbg: e.matmul(
                        pbg[:, :], self.abuf[:, slg, c, :],
                        self.ln[:, c, u * 512:(u + 1) * 512],
                        start=(c == 0), stop=(c == DC - 1)),
                        waits=cw,
                        inc=self.pe if c == DC - 1 else None)
                mm_g = self.pe.n
                for c in range(DC):
                    cw = list(w_u) if c == 0 else []
                    if j == 0 and ln_entry is not None:
                        cw.append(self.lnrdy[(u, c)])
                    p.emit("tensor", lambda e, slu=slu, u=u, c=c, pbu=pbu: e.matmul(
                        pbu[:, :], self.abuf[:, slu, c, :],
                        self.ln[:, c, u * 512:(u + 1) * 512],
                        start=(c == 0), stop=(c == DC - 1)),
                        waits=cw,
                        inc=self.pe if c == DC - 1 else None)
                mm_u = self.pe.n
                if u == NU - 1:
                    self.a_war[slg] = (self.pe, mm_g)
                    self.a_war[slu] = (self.pe, mm_u)
                ssl = (j * NU + u) % 2
                sg_w = [(self.pe, mm_g)]
                if ssl in self.hmul_done:
                    sg_w.append((self.dve, self.hmul_done[ssl]))
                ma = p.emit("scalar", lambda e, j=j, ssl=ssl, pbg=pbg: e.activation(
                    self.sg_scr[:, ssl, :], pbg[:, :], AF.Silu, scale=self.gs_sb[:, j:j + 1]),
                    waits=sg_w, inc=self.act)
                ev_done[key_g] = ma
                md = p.emit("vector", lambda e, j=j, ssl=ssl, pbu=pbu: e.tensor_scalar(
                    self.ut_scr[:, ssl, :], pbu[:, :], self.us_sb[:, j:j + 1], None, ALU.mult),
                    waits=[(self.pe, mm_u)], inc=self.dve)
                ev_done[key_u] = md
                pend_h.append((j, u, ssl, ma, md))
                flush_h(1)
        flush_h(0)
        self.bar()

    def fdn(self, g, xsrc, xdst, shadow=False):
        p = self.p
        if not hasattr(self, "sqbf_war"):
            self.sqbf_war = {}
        b, toff = g // 2, (g % 2) * TG
        ev_done = {}
        stt_done = {}
        st_done = {}
        for j in range(DC):
            hm = []
            for half in range(2):
                slot = self.b_ctr % 3
                self.b_ctr += 1
                w = list(self.b_extra_w)
                self.b_extra_w = []
                if slot in self.b_war:
                    w.append(self.b_war.pop(slot))
                mw = p.emit("gpsimd", lambda e, j=j, slot=slot, half=half: e.dma_start(
                    self.dbuf_h[:, slot, :, :],
                    self.dw[j, :, half * 2816:(half + 1) * 2816]
                        .rearrange("p (c i) -> p c i", c=22)),
                    waits=w, inc=self.sb[slot], inc_n=16)
                hm.append((slot, mw))
            for u in range(NU):
                r = (j * NU + u) % 4
                sl = slice(toff + u * 512, toff + u * 512 + 512)
                ld_w = []
                if r in stt_done:
                    ld_w.append((self.dve, stt_done[r]))
                mx = p.emit("sync", lambda e, j=j, r=r, sl=sl: e.dma_start(
                    self.xin_fd[:, r, :], xsrc[b, j * 128:(j + 1) * 128, sl]),
                    waits=ld_w, inc=self.sxi[r], inc_n=16)
                par = (j * NU + u) % 3
                pbd = self.pb[par]
                w0 = [(self.sb[hm[0][0]], hm[0][1])]
                if par in ev_done:
                    w0.append((self.dve, ev_done[par]))
                m_halfa = None
                for c in range(IC):
                    half = 0 if c < 22 else 1
                    slot, mw = hm[half]
                    cw = w0 if c == 0 else (
                        [(self.sb[slot], mw)] if c == 22 else ())
                    want_inc = (c == IC - 1) or (c == 21 and u == NU - 1)
                    mk = p.emit("tensor", lambda e, slot=slot, u=u, c=c, half=half, pbd=pbd: e.matmul(
                        pbd[:, :], self.dbuf_h[:, slot, c - half * 22, :],
                        self.h[:, c, u * 512:(u + 1) * 512],
                        start=(c == 0), stop=(c == IC - 1)),
                        waits=cw,
                        inc=self.pe if want_inc else None)
                    if c == 21 and u == NU - 1:
                        m_halfa = mk
                mm = self.pe.n
                if u == NU - 1:
                    self.b_war[hm[0][0]] = (self.pe, m_halfa)
                    self.b_war[hm[1][0]] = (self.pe, mm)
                stt_w = [(self.pe, mm), (self.sxi[r], mx)]
                if r in st_done:
                    stt_w.append((self.sso[r], st_done[r]))
                if r in self.sh_done:
                    stt_w.append(self.sh_done.pop(r))
                me = p.emit("vector", lambda e, j=j, r=r, pbd=pbd: e.scalar_tensor_tensor(
                    self.xout_fd[:, r, :], pbd[:, :], self.dsh_sb[:, j:j + 1],
                    self.xin_fd[:, r, :], ALU.mult, ALU.add),
                    waits=stt_w, inc=self.dve)
                ev_done[par] = me
                stt_done[r] = me
                st_done[r] = p.emit("scalar", lambda e, j=j, r=r, sl=sl: e.dma_start(
                    xdst[b, j * 128:(j + 1) * 128, sl], self.xout_fd[:, r, :]),
                    waits=[(self.dve, me)], inc=self.sso[r], inc_n=16)
                if shadow:
                    self.sh_done[r] = self._shadow_unit(j, u, r, self.xout_fd, me)
                    self._shadow_flush(2)
        self._shadow_flush(0)
        self.bar()

    def _shadow_unit(self, j, u, r, xout, me):
        p = self.p
        slot = (j * NU + u) % 4
        msh = p.emit("scalar", lambda e, j=j, u=u, r=r: e.activation(
            self.ln[:, j, u * 512:(u + 1) * 512], xout[:, r, :], AF.Copy),
            waits=[(self.dve, me)], inc=self.act)
        sq_w = [(self.act, msh)]
        if slot in self.sqbf_war:
            sq_w.append(self.sqbf_war[slot])
        for c_ in self.smisc:
            if c_.n > 0:
                sq_w.append((c_, c_.n))
        msq_a = p.emit("scalar", lambda e, j=j, u=u, slot=slot: e.activation(
            self.sqbf[:, slot, :], self.ln[:, j, u * 512:(u + 1) * 512], AF.Square),
            waits=sq_w, inc=self.act)
        self._shadow_q.append((j, u, slot, msh, msq_a))
        return (self.act, msh)

    def _shadow_mms(self, j, u, slot, msh, msq_a):
        p = self.p
        pbm = self.pb[3] if u == 0 else self.pb[5]
        pbq = self.pb[4] if u == 0 else self.pb_pl
        p.emit("tensor", lambda e, j=j, u=u, pbm=pbm: e.matmul(
            pbm[:, :], self.invd_sb[:, :], self.ln[:, j, u * 512:(u + 1) * 512],
            start=(j == 0), stop=(j == DC - 1), skip_group_check=True),
            waits=[(self.act, msh)], inc=self.pe)
        mq = p.emit("tensor", lambda e, j=j, slot=slot, pbq=pbq: e.matmul(
            pbq[:, :], self.invd_sb[:, :], self.sqbf[:, slot, :],
            start=(j == 0), stop=(j == DC - 1), skip_group_check=True),
            waits=[(self.act, msq_a)], inc=self.pe)
        self.sqbf_war[slot] = (self.pe, mq)

    def _shadow_flush(self, lag):
        while len(self._shadow_q) > lag:
            self._shadow_mms(*self._shadow_q.pop(0))


    def qkv(self, g, ln_entry=None):
        self.p.cur_label = "qkv"
        p = self.p
        self._qkv_ln_entry = ln_entry
        b, toff = g // 2, (g % 2) * TG
        batch_start = (g % 2 == 0)
        mtbl = p.emit("sync", lambda e: e.dma_start(
            self.tbl_sb[:, :, :], self.c_tbl[:, :, toff:toff + TG]),
            inc=self.xd, inc_n=16)
        if not batch_start:
            p.emit("sync", lambda e: e.dma_start(
                self.kbuf[:, :, 0:WIN], self.kprev_d[:, :, :]),
                waits=[(self.smisc[0], self.smisc[0].n), (self.smisc[1], self.smisc[1].n),
                       (self.pe, self.pe.n), (self.dve, self.dve.n), (self.act, self.act.n)],
                inc=self.xd, inc_n=16)
            p.emit("sync", lambda e: e.dma_start(
                self.vbuf[:, 0:2, :], self.vprev_d[:, :, :]),
                inc=self.xd, inc_n=16)

        slabs = ([("q", j) for j in range(DC)] + [("k", j) for j in range(DC)]
                 + [("v", j) for j in range(DC)])
        bank_free = {}
        rot_free = {}
        rp_free = [None]
        qtmp_free = {}
        vfm_free = {}
        pbt_free = {}
        srcs = {"q": self.wq, "k": self.wk, "v": self.wv}
        rotbanks = [self.pb[4], self.pb[5], self.pb_pl]
        deferred = []
        unit = 0

        def qk_bundle(kind, j, u, tslot, rslot, mc):
            def go():
                ti = 2 if kind == "q" else 0
                pbr = rotbanks[rslot]
                w_r = [(self.act, mc)]
                if rslot in rot_free:
                    w_r.append(rot_free[rslot])
                elif self._qkv_ln_entry is not None:
                    bk = [("pb", 4), ("pb", 5), ("pl",)][rslot]
                    if bk in self.ln_bankfree:
                        w_r.append(self.ln_bankfree[bk])
                    w_r.extend(self._qkv_ln_entry)
                mr = p.emit("tensor", lambda e, tslot=tslot, pbr=pbr: e.matmul(
                    pbr[:, :], self.rotm_sb[:, :], self.qtmp_sb[:, tslot, :],
                    start=True, stop=True),
                    waits=w_r, inc=self.pe)
                w_a1 = [(self.act, mc), (self.xd, mtbl)]
                if rp_free[0] is not None:
                    w_a1.append(rp_free[0])
                ma1 = p.emit("vector", lambda e, tslot=tslot, ti=ti, u=u: e.tensor_mul(
                    self.rp_a[:, :], self.qtmp_sb[:, tslot, :],
                    self.tbl_sb[:, ti, u * 512:(u + 1) * 512]),
                    waits=w_a1, inc=self.dve)
                ma2 = p.emit("vector", lambda e, pbr=pbr, ti=ti, u=u: e.tensor_mul(
                    self.rp_b[:, :], pbr[:, :],
                    self.tbl_sb[:, ti + 1, u * 512:(u + 1) * 512]),
                    waits=[(self.pe, mr)], inc=self.dve)
                rot_free[rslot] = (self.dve, ma2)
                if kind == "q":
                    dst = self.qrot[:, j, u * 512:(u + 1) * 512]
                else:
                    dst = self.kbuf[:, j, WIN + u * 512: WIN + (u + 1) * 512]
                madd = p.emit("gpsimd", lambda e, dst=dst: e.tensor_add(
                    dst, self.rp_a[:, :], self.rp_b[:, :]),
                    waits=[(self.dve, ma2)], inc=self.gp)
                rp_free[0] = (self.gp, madd)
                qtmp_free[tslot] = [(self.dve, ma1), (self.pe, mr)]
            return go

        def v_bundle(j, u, vslot, ph, mc):
            def go():
                w_t = [(self.act, mc)]
                if ph in pbt_free:
                    w_t.append(pbt_free[ph])
                mtt = None
                for tc in range(4):
                    mtt = p.emit("tensor", lambda e, ph=ph, tc=tc, vslot=vslot: e.transpose(
                        self.pb_t[:, ph * 512 + tc * 128: ph * 512 + (tc + 1) * 128],
                        self.vfm_sb[:, vslot, tc * 128:(tc + 1) * 128],
                        self.ident_sb[:, :]),
                        waits=w_t if tc == 0 else (),
                        inc=self.pe)
                vfm_free[vslot] = (self.pe, mtt)
                mdv = p.emit("vector", lambda e, ph=ph, u=u, j=j: e.tensor_copy(
                    self.vbuf[:, 2 + u * 4: 2 + u * 4 + 4, j * 128:(j + 1) * 128],
                    bass.AP(self.pb_t, ph * 512, [[1024, 128], [128, 4], [1, 128]])),
                    waits=[(self.pe, mtt)], inc=self.dve)
                pbt_free[ph] = (self.dve, mdv)
            return go

        banks_seen = set()
        for s_idx, (kind, j) in enumerate(slabs):
            slot, msl = self._aload(srcs[kind][j, :, :])
            for u in range(NU):
                while deferred and deferred[0][0] <= unit - 2:
                    deferred.pop(0)[1]()
                bank = (2 * s_idx + u) % 4
                pb = self.pb[bank]
                w0 = [(self.sa[slot], msl)]
                if bank in bank_free:
                    w0.append(bank_free[bank])
                elif ln_entry is not None and bank not in banks_seen:
                    bk = ("pb", bank)
                    if bk in self.ln_bankfree:
                        w0.append(self.ln_bankfree[bk])
                    w0.extend(ln_entry)
                banks_seen.add(bank)
                for c in range(DC):
                    cw = list(w0) if c == 0 else []
                    if s_idx == 0 and ln_entry is not None:
                        cw.append(self.lnrdy[(u, c)])
                    mmark = p.emit("tensor", lambda e, slot=slot, c=c, u=u, pb=pb: e.matmul(
                        pb[:, :], self.abuf[:, slot, c, :],
                        self.ln[:, c, u * 512:(u + 1) * 512],
                        start=(c == 0), stop=(c == DC - 1)),
                        waits=cw,
                        inc=self.pe)
                if u == NU - 1:
                    self.a_war[slot] = (self.pe, mmark)
                if kind in ("q", "k"):
                    tslot = unit % 4
                    w_c = [(self.pe, mmark)]
                    if tslot in qtmp_free:
                        w_c.extend(qtmp_free[tslot])
                    mc = p.emit("scalar", lambda e, tslot=tslot, pb=pb: e.activation(
                        self.qtmp_sb[:, tslot, :], pb[:, :], AF.Copy),
                        waits=w_c, inc=self.act)
                    bank_free[bank] = (self.act, mc)
                    deferred.append((unit, qk_bundle(kind, j, u, tslot, unit % 3, mc)))
                else:
                    vslot = unit % 2
                    w_c = [(self.pe, mmark)]
                    if vslot in vfm_free:
                        w_c.append(vfm_free[vslot])
                    mc = p.emit("scalar", lambda e, vslot=vslot, pb=pb: e.activation(
                        self.vfm_sb[:, vslot, :], pb[:, :], AF.Copy),
                        waits=w_c, inc=self.act)
                    bank_free[bank] = (self.act, mc)
                    deferred.append((unit, v_bundle(j, u, vslot, unit % 2, mc)))
                unit += 1
        while deferred:
            deferred.pop(0)[1]()
        self.bar()

    def core(self, g):
        self.p.cur_label = "core"
        p = self.p
        NWH = 4 * NH
        marks = {}

        def geom(widx):
            w, h = widx // NH, widx % NH
            wa = (g % 2) * 4 + w
            return w, h, wa > 0

        def emit_exp(widx):
            _, _, has_prev = geom(widx)
            nch = 4 if has_prev else 2
            sl = widx % 4
            w_e = [marks[("negb", widx)]] + list(marks.get(("psb_slot", sl), []))
            if nch == 4:
                w_e.append(marks[("ev_prev", widx)])
            mex = p.emit("scalar", lambda e, sl=sl, nch=nch: e.activation(
                self.psb[:, sl, 0:nch * 256], self.scores[:, sl, 0:nch * 256],
                AF.Exp, bias=self.negb_sb[:, sl:sl + 1]),
                waits=w_e, inc=self.act)
            marks[("scores_slot", sl)] = [(self.act, mex)]
            marks[("exp", widx)] = (self.act, mex)

        def emit_scores(widx):
            w, h, has_prev = geom(widx)
            nch = 4 if has_prev else 2
            npair = nch // 2
            sl = widx % 4
            bset = 4 if (widx % 2 == 0) else 2
            for pr in range(npair):
                pbs = self.pb[bset + pr]
                w0 = list(marks.get(("scfree", bset + pr), []))
                for kk in range(2):
                    kc = pr * 2 + kk
                    kcol = w * 256 + (0 if has_prev else 256) + kc * 128
                    mmark = p.emit("tensor", lambda e, pbs=pbs, kk=kk, h=h, kcol=kcol, w=w: e.matmul(
                        pbs[:, kk * 256:(kk + 1) * 256],
                        self.kbuf[:, h, kcol:kcol + 128],
                        self.qrot[:, h, w * 256:(w + 1) * 256],
                        start=(kk == 0), stop=(kk == 1), skip_group_check=True),
                        waits=w0 if kk == 0 else (),
                        inc=self.pe)
                marks[("scmm", widx, pr)] = (self.pe, mmark)

        def emit_gmax(widx):
            _, _, has_prev = geom(widx)
            npair = 2 if has_prev else 1
            sl = widx % 4
            w_n = [marks[("evac_dve", widx)]] + list(marks.get(("nmb_slot", sl), []))
            mnm = p.emit("gpsimd", lambda e, sl=sl, npair=npair: e.tensor_reduce(
                self.nmb_sb[0:1, sl:sl + 1],
                self.scores[:, sl, (npair - 1) * 512:npair * 512],
                AX.XYZWC, ALU.max),
                waits=w_n, inc=self.gp)
            marks[("nm", widx)] = (self.gp, mnm)

        def emit_bcast(widx):
            sl = widx % 4
            w_b = [marks[("nm", widx)]] + list(marks.get(("pbpl", sl), []))
            mbc = p.emit("tensor", lambda e, sl=sl: e.matmul(
                self.pb_pl[:, sl:sl + 1], self.negones_sb[0:1, :],
                self.nmb_sb[0:1, sl:sl + 1],
                start=True, stop=True, skip_group_check=True),
                waits=w_b, inc=self.pe)
            marks[("nmb_slot", sl)] = [(self.pe, mbc)]
            marks[("bc", widx)] = (self.pe, mbc)

        def emit_negb(widx):
            sl = widx % 4
            mng = p.emit("scalar", lambda e, sl=sl: e.activation(
                self.negb_sb[:, sl:sl + 1], self.pb_pl[:, sl:sl + 1], AF.Copy),
                waits=[marks[("bc", widx)]], inc=self.act)
            marks[("pbpl", sl)] = [(self.act, mng)]
            marks[("negb", widx)] = (self.act, mng)

        def emit_evac(widx):
            _, _, has_prev = geom(widx)
            nch = 4 if has_prev else 2
            npair = nch // 2
            sl = widx % 4
            bset = 4 if (widx % 2 == 0) else 2
            for pr in range(npair):
                pbs = self.pb[bset + pr]
                wv = [marks[("scmm", widx, pr)]] + list(marks.get(("scores_slot", sl), []))
                if pr == npair - 1:
                    me = p.emit("vector", lambda e, pbs=pbs, sl=sl, pr=pr: e.tensor_add(
                        self.scores[:, sl, pr * 512:(pr + 1) * 512], pbs[:, :],
                        bass.AP(self.mask23_sb, 0, [[512, 128], [1, 512]])),
                        waits=wv, inc=self.dve)
                    marks[("scfree", bset + pr)] = [(self.dve, me)]
                    marks[("evac_dve", widx)] = (self.dve, me)
                else:
                    me = p.emit("vector", lambda e, pbs=pbs, sl=sl, pr=pr: e.tensor_copy(
                        self.scores[:, sl, pr * 512:(pr + 1) * 512], pbs[:, :]),
                        waits=wv, inc=self.dve)
                    marks[("scfree", bset + pr)] = [(self.dve, me)]
                    marks[("ev_prev", widx)] = (self.dve, me)

        def emit_denpv(widx):
            w, h, has_prev = geom(widx)
            nch = 4 if has_prev else 2
            sl = widx % 4
            hb = widx % 2
            pbb = self.pb[hb]
            w_d = [marks[("exp", widx)]] + list(marks.get(("denout", hb), []))
            for kc in range(nch):
                p.emit("tensor", lambda e, pbb=pbb, sl=sl, kc=kc, nch=nch: e.matmul(
                    pbb[:, 0:256], self.ones_sb[:, :],
                    self.psb[:, sl, kc * 256:(kc + 1) * 256],
                    start=(kc == 0), stop=(kc == nch - 1), skip_group_check=True),
                    waits=w_d if kc == 0 else (),
                    inc=self.pe)
            for kc in range(nch):
                vidx = (2 * w + kc) if has_prev else (2 * w + 2 + kc)
                mpv = p.emit("tensor", lambda e, pbb=pbb, sl=sl, kc=kc, vidx=vidx, h=h, nch=nch: e.matmul(
                    pbb[:, 256:512], self.vbuf[:, vidx, h * 128:(h + 1) * 128],
                    self.psb[:, sl, kc * 256:(kc + 1) * 256],
                    start=False, stop=(kc == nch - 1), skip_group_check=True),
                    inc=self.pe)
            marks[("psb_slot", sl)] = [(self.pe, mpv)]
            marks[("denpv", widx)] = (self.pe, mpv)

        def emit_rcscr(widx):
            sl = widx % 4
            hb = widx % 2
            pbb = self.pb[hb]
            w_r = [marks[("denpv", widx)]] + list(marks.get(("rcln_slot", sl), []))
            mr1 = p.emit("scalar", lambda e, pbb=pbb, sl=sl: e.activation(
                self.rcscr[:, sl, :], pbb[:, 0:256], AF.Ln, bias=self.epsden_sb[:, :]),
                waits=w_r, inc=self.act)
            marks[("rcscr", widx)] = (self.act, mr1)

        def emit_rc_ao(widx):
            w, h, has_prev = geom(widx)
            npair = 2 if has_prev else 1
            sl = widx % 4
            hb = widx % 2
            pbb = self.pb[hb]
            w_r2 = [marks[("rcscr", widx)]] + list(marks.get(("rc_slot", sl), []))
            mr2 = p.emit("scalar", lambda e, sl=sl: e.activation(
                self.rc[:, sl, :], self.rcscr[:, sl, :], AF.Exp, scale=-1.0),
                waits=w_r2, inc=self.act)
            marks[("rcln_slot", sl)] = [(self.act, mr2)]
            w_a = [(self.act, mr2), marks[("scmm", widx, npair - 1)]]
            mao = p.emit("vector", lambda e, h=h, w=w, sl=sl, pbb=pbb: e.tensor_mul(
                self.qrot[:, h, w * 256:(w + 1) * 256], pbb[:, 256:512],
                self.rc[:, sl, :]),
                waits=w_a, inc=self.dve)
            marks[("rc_slot", sl)] = [(self.dve, mao)]
            marks[("denout", hb)] = [(self.act, marks[("rcscr", widx)][1]), (self.dve, mao)]
            marks[("ao", widx)] = (self.dve, mao)

        for t in range(NWH + 5):
            if 0 <= t - 2 < NWH:
                emit_exp(t - 2)
            if t < NWH:
                emit_scores(t)
            if 0 <= t - 1 < NWH:
                emit_bcast(t - 1)
            if 0 <= t - 4 < NWH:
                emit_rc_ao(t - 4)
            if 0 <= t - 1 < NWH:
                emit_negb(t - 1)
            if t < NWH:
                emit_evac(t)
                emit_gmax(t)
            if 0 <= t - 3 < NWH:
                emit_denpv(t - 3)
                emit_rcscr(t - 3)
        self.bar()

    def oproj_preload(self, qkv_end):
        self.wo_pre = {}
        for j in range(6):
            self.wo_pre[j] = self._aload(self.wo[j, :, :],
                                         extra_w=[(self.pe, qkv_end)])

    def oproj(self, g, xsrc, xdst, shadow=False):
        p = self.p
        b, toff = g // 2, (g % 2) * TG
        if g % 2 == 0:
            p.emit("sync", lambda e: e.dma_start(
                self.kprev_d[:, :, :], self.kbuf[:, :, TG:TG + WIN]),
                inc=self.smisc[0], inc_n=16)
            p.emit("sync", lambda e: e.dma_start(
                self.vprev_d[:, :, :], self.vbuf[:, 8:10, :]),
                inc=self.smisc[1], inc_n=16)
        bank_free = {}
        xin_done = {}
        st_done = {}
        for j in range(DC):
            if hasattr(self, "wo_pre") and j in self.wo_pre:
                slot, msl = self.wo_pre.pop(j)
            else:
                slot, msl = self._aload(self.wo[j, :, :])
            for u in range(NU):
                r = (j * NU + u) % 4
                sl = slice(toff + u * 512, toff + u * 512 + 512)
                ld_w = []
                if r in xin_done:
                    ld_w.append((self.dve, xin_done[r]))
                mx = p.emit("sync", lambda e, j=j, r=r, sl=sl: e.dma_start(
                    self.xin_fo[:, r, :], xsrc[b, j * 128:(j + 1) * 128, sl]),
                    waits=ld_w, inc=self.sxi[r], inc_n=16)
                bank = (j * NU + u) % (3 if shadow else 4)
                pb = self.pb[bank]
                w0 = [(self.sa[slot], msl)]
                if bank in bank_free:
                    w0.append(bank_free[bank])
                for c in range(DC):
                    mmark = p.emit("tensor", lambda e, slot=slot, c=c, u=u, pb=pb: e.matmul(
                        pb[:, :], self.abuf[:, slot, c, :],
                        self.ao_sb[:, c, u * 512:(u + 1) * 512],
                        start=(c == 0), stop=(c == DC - 1)),
                        waits=w0 if c == 0 else (),
                        inc=self.pe)
                if u == NU - 1:
                    self.a_war[slot] = (self.pe, mmark)
                stt_w = [(self.pe, mmark), (self.sxi[r], mx)]
                if r in st_done:
                    stt_w.append((self.sso[r], st_done[r]))
                if r in self.sh_done:
                    stt_w.append(self.sh_done.pop(r))
                me = p.emit("vector", lambda e, r=r, pb=pb: e.tensor_add(
                    self.xout_fo[:, r, :], pb[:, :], self.xin_fo[:, r, :]),
                    waits=stt_w, inc=self.dve)
                bank_free[bank] = (self.dve, me)
                xin_done[r] = me
                st_done[r] = p.emit("scalar", lambda e, j=j, r=r, sl=sl: e.dma_start(
                    xdst[b, j * 128:(j + 1) * 128, sl], self.xout_fo[:, r, :]),
                    waits=[(self.dve, me)], inc=self.sso[r], inc_n=16)
                if shadow:
                    self.sh_done[r] = self._shadow_unit(j, u, r, self.xout_fo, me)
                    self._shadow_flush(2)
        self._shadow_flush(0)
        self.bar()


    def build(self):
        self.prep()
        pe = self.pe
        prev_dn_end = 0
        g2_end_prev = 0
        for g in range(NG):
            self.soften(gpsimd=[(pe, g2_end_prev)])
            self.lnin(g, self.xin)
            l1_end = pe.n
            self.soften(gpsimd=[], tensor=[])
            self.fgu(ln_entry=[])
            fgu_end = pe.n
            if self.stage == 1:
                self.b_extra_w = [(pe, l1_end)]
                self.fdn(g, self.xin, self.y)
                continue
            self.soften(sync=[(pe, fgu_end), (self.dve, self.dve.n),
                              (self.act, self.act.n)])
            self.b_extra_w = [(pe, l1_end)]
            self.fdn(g, self.xin, self.x1, shadow=True)
            dn1_end = pe.n
            dn1_dve = self.dve.n
            self.lnin(g)
            self.soften(gpsimd=[], tensor=[],
                        sync=[(pe, dn1_end), (self.dve, self.dve.n)] + self._sso_totals())
            self.qkv(g, ln_entry=[(self.dve, dn1_dve)])
            qkv_end = pe.n
            self.oproj_preload(qkv_end)
            self.soften(sync=[])
            self.core(g)
            core_end = pe.n
            self.soften(gpsimd=[(pe, qkv_end)],
                        sync=[(pe, core_end)] + self._sso_totals())
            if self.stage == 2:
                self.oproj(g, self.x1, self.y)
                continue
            self.oproj(g, self.x1, self.x2, shadow=True)
            op_end = pe.n
            op_dve = self.dve.n
            self.lnin(g)
            self.soften(gpsimd=[], tensor=[])
            self.fgu(ln_entry=[(self.dve, op_dve)])
            g2_end_prev = pe.n
            self.soften(sync=[(pe, pe.n), (self.dve, self.dve.n),
                              (self.act, self.act.n)] + self._sso_totals())
            self.b_extra_w = ([(pe, op_end)]
                              + [(c, c.n) for c in self.smisc if c.n > 0])
            self.fdn(g, self.x2, self.y)
            prev_dn_end = pe.n
        fin = [(c, c.n) for c in self.sso if c.n > 0]
        if self.sd.n > 0:
            fin.append((self.sd, self.sd.n))
        self.p.emit("scalar", None, waits=fin)
        build_block(self.nc, self.p)



def _prep_weights(inputs):
    gw = inputs["L0_ffn_gate_w"]
    uw = inputs["L0_ffn_up_w"]
    dw = inputs["L0_ffn_down_w"]
    arr = {}
    arr["gw"] = np.ascontiguousarray(gw.reshape(IC, 128, DC, 128).transpose(0, 3, 2, 1).reshape(IC, 128, DC * 128))
    arr["uw"] = np.ascontiguousarray(uw.reshape(IC, 128, DC, 128).transpose(0, 3, 2, 1).reshape(IC, 128, DC * 128))
    arr["dw"] = np.ascontiguousarray(dw.reshape(DC, 128, IC, 128).transpose(0, 3, 2, 1).reshape(DC, 128, IC * 128))
    arr["gsd"] = np.ascontiguousarray(inputs["L0_ffn_gate_s"].reshape(IC, 128).T)
    arr["usd"] = np.ascontiguousarray(inputs["L0_ffn_up_s"].reshape(IC, 128).T)
    arr["dsd"] = np.ascontiguousarray(inputs["L0_ffn_down_s"].reshape(DC, 128).T)
    for nm, w in [("wqt", inputs["L0_attn_Wq"]), ("wkt", inputs["L0_attn_Wk"]),
                  ("wvt", inputs["L0_attn_Wv"]), ("wot", inputs["L0_attn_Wo"])]:
        arr[nm] = np.ascontiguousarray(
            np.asarray(w).reshape(DC, 128, DC, 128).transpose(2, 1, 0, 3).reshape(DC, 128, DC * 128))
    return arr


def kernel(**inputs):
    hs = np.asarray(inputs["L0_attn_hidden_states"])
    warr = _prep_weights({k: np.asarray(v) for k, v in inputs.items()})

    nc = bass.Bass("TRN2", target_bir_lowering=False)
    mk = MK(nc)
    mk.build()

    in_maps = []
    for core in range(N_CORES):
        xin_c = np.ascontiguousarray(hs[2 * core:2 * core + 2].transpose(0, 2, 1))
        m = {"xin": xin_c}
        m.update(warr)
        in_maps.append(m)

    res = run_bass_kernel_spmd(nc, in_maps, core_ids=list(range(N_CORES)))
    out = np.empty((16, S, D), dtype=np.float32)
    for core in range(N_CORES):
        out[2 * core:2 * core + 2] = res.results[core]["y"].transpose(0, 2, 1)
    return out


if __name__ == "__main__":
    pass


# version 16
# speedup vs baseline: 1.0463x; 1.0463x over previous
import math
import numpy as np
import ml_dtypes
import concourse.bass as bass
import concourse.mybir as mybir
from concourse.bass_utils import run_bass_kernel_spmd

F32 = mybir.dt.float32
BF16 = mybir.dt.bfloat16
AF = mybir.ActivationFunctionType
ALU = mybir.AluOpType
AX = mybir.AxisListType

D = 2048
DC = 16
I_DIM = 5632
IC = 44
S = 2048
B_CORE = 2
TG = 1024
NG = 4
NU = 2
WIN = 256
NH = 16
HD = 128
NEG = -1e30
N_CORES = 8

STAGE = 3


class Ctr:
    __slots__ = ("h", "n", "name")
    def __init__(self, h, name):
        self.h = h
        self.n = 0
        self.name = name

class Prog:
    ENGS = ("tensor", "vector", "scalar", "sync", "gpsimd")

    def __init__(self):
        self.ops = {e: [] for e in self.ENGS}
        self.pending = {e: [] for e in self.ENGS}
        self.seen = {e: {} for e in self.ENGS}
        self.since_barrier = {e: None for e in self.ENGS}

    def emit(self, eng, fn, waits=(), inc=None, inc_n=1):
        wl = self.pending[eng]
        self.pending[eng] = []
        for (c, v) in waits:
            wl.append((c, v))
        out_waits = []
        for (c, v) in wl:
            prev = self.seen[eng].get(c.h.num, -1)
            if v > prev:
                self.seen[eng][c.h.num] = v
                out_waits.append((c.h, v))
        incs = []
        ret = None
        if inc is not None:
            inc.n += inc_n
            incs.append((inc.h, inc_n))
            ret = inc.n
        self.ops[eng].append([fn, out_waits, incs])
        self.since_barrier[eng] = len(self.ops[eng]) - 1
        return ret

    def ensure_inc(self, eng, ctr):
        idx = self.since_barrier[eng]
        if idx is None:
            return
        rec = self.ops[eng][idx]
        for (h, n) in rec[2]:
            if h.num == ctr.h.num:
                return
        ctr.n += 1
        rec[2].append((ctr.h, 1))

    def barrier(self, eng_ctrs, all_ctrs):
        snap = [(c, c.n) for c in all_ctrs if c.n > 0]
        for e in self.ENGS:
            own = eng_ctrs.get(e, ())
            own_nums = {c.h.num for c in own}
            for (c, v) in snap:
                if c.h.num not in own_nums:
                    self.pending[e].append((c, v))
            self.since_barrier[e] = None


def build_block(nc, prog):
    with nc.Block() as block:
        for eng_name in Prog.ENGS:
            ops = prog.ops[eng_name]
            def runner(eng, ops=ops):
                for fn, waits, incs in ops:
                    for (h, v) in waits:
                        eng.wait_ge(h, v)
                    ins = fn(eng) if fn is not None else None
                    if incs:
                        assert ins is not None
                        for (h, n) in incs:
                            ins.then_inc(h, n)
            getattr(block, eng_name)(runner)



def make_tables():
    inv_freq = (10000.0 ** (-(np.arange(0, 64, dtype=np.float32)) / 64.0)).astype(np.float32)
    t = np.arange(S, dtype=np.float32)
    ang = (t[:, None] * inv_freq[None, :]).astype(np.float32)
    cos = np.cos(ang).astype(np.float32).T
    sin = np.sin(ang).astype(np.float32).T
    sc = np.float32(1.0 / math.sqrt(HD))
    cosf = np.concatenate([cos, cos], axis=0)
    sinf = np.concatenate([sin, sin], axis=0)
    tbl = np.stack([cosf, sinf, cosf * sc, sinf * sc], axis=1)
    j = np.arange(256)[:, None]
    q = np.arange(256)[None, :]
    mask23 = np.where(j <= q, 0.0, NEG).astype(np.float32)
    m = np.zeros((128, 2, 256), dtype=np.float32)
    m[:, 0, :] = mask23[0:128, :]
    m[:, 1, :] = mask23[128:256, :]
    rotm = np.zeros((128, 128), dtype=ml_dtypes.bfloat16)
    for d in range(64):
        rotm[d + 64, d] = -1.0
    for d in range(64, 128):
        rotm[d - 64, d] = 1.0
    return np.ascontiguousarray(tbl), np.ascontiguousarray(m), rotm



ARENA = 163840
WB = 24576

class MK:
    def __init__(self, nc, stage=STAGE):
        self.nc = nc
        self.stage = stage
        self.p = Prog()

        dt = nc.dram_tensor
        self.xin = dt("xin", [B_CORE, D, S], F32, kind="ExternalInput").ap()
        self.gw = dt("gw", [IC, 128, DC * 128], F32, kind="ExternalInput").ap()
        self.uw = dt("uw", [IC, 128, DC * 128], F32, kind="ExternalInput").ap()
        self.dw = dt("dw", [DC, 128, IC * 128], F32, kind="ExternalInput").ap()
        self.gsd = dt("gsd", [128, IC], F32, kind="ExternalInput").ap()
        self.usd = dt("usd", [128, IC], F32, kind="ExternalInput").ap()
        self.dsd = dt("dsd", [128, DC], F32, kind="ExternalInput").ap()
        self.wq = dt("wqt", [DC, 128, DC * 128], F32, kind="ExternalInput").ap()
        self.wk = dt("wkt", [DC, 128, DC * 128], F32, kind="ExternalInput").ap()
        self.wv = dt("wvt", [DC, 128, DC * 128], F32, kind="ExternalInput").ap()
        self.wo = dt("wot", [DC, 128, DC * 128], F32, kind="ExternalInput").ap()
        self.y = dt("y", [B_CORE, D, S], F32, kind="ExternalOutput").ap()
        self.x1 = dt("x1", [B_CORE, D, S], F32).ap()
        self.x2 = dt("x2", [B_CORE, D, S], F32).ap()
        self.kprev_d = dt("kprev_d", [128, NH, WIN], BF16).ap()
        self.vprev_d = dt("vprev_d", [128, 2, D], BF16).ap()

        tbl, mask23, rotm = make_tables()
        it = nc.inline_tensor
        self.c_tbl = it(tbl, name="c_tbl").ap()
        self.c_mask23 = it(mask23, name="c_mask23").ap()
        self.c_rotm = it(rotm, name="c_rotm").ap()
        self.c_id128f = it(np.eye(128, dtype=np.float32), name="c_id128f").ap()
        self.c_ident = it(np.eye(128, dtype=ml_dtypes.bfloat16), name="c_ident").ap()
        self.c_ones = it(np.ones((128, 128), dtype=ml_dtypes.bfloat16), name="c_ones").ap()
        self.c_negones = it(np.full((1, 128), -1.0, dtype=ml_dtypes.bfloat16), name="c_negones").ap()
        self.c_invd = it(np.full((128, 128), 1.0 / D, dtype=ml_dtypes.bfloat16), name="c_invd").ap()
        self.c_ones16 = it(np.ones((16, 128), dtype=ml_dtypes.bfloat16), name="c_ones16").ap()
        self.c_id16 = it(np.eye(16, dtype=np.float32), name="c_id16").ap()
        self.c_eps = it(np.full((128, 1), 1e-5, dtype=np.float32), name="c_eps").ap()
        self.c_epsden = it(np.full((128, 1), 1e-20, dtype=np.float32), name="c_epsden").ap()

        a = nc.alloc_sbuf_tensor
        self.ident_sb = a("ident_sb", [128, 128], BF16)
        self.ones_sb = a("ones_sb", [128, 128], BF16)
        self.negones_sb = a("negones_sb", [1, 128], BF16)
        self.invd_sb = a("invd_sb", [128, 128], BF16)
        self.ones16_sb = a("ones16_sb", [16, 128], BF16)
        self.id16_sb = a("id16_sb", [16, 16], F32)
        self.eps_sb = a("eps_sb", [128, 1], F32)
        self.epsden_sb = a("epsden_sb", [128, 1], F32)
        self.mask23_sb = a("mask23_sb", [128, 2, WIN], F32)
        self.rotm_sb = a("rotm_sb", [128, 128], BF16)
        self.id128f_sb = a("id128f_sb", [128, 128], F32)
        self.qtmp_sb = a("qtmp_sb", [128, 4, 512], BF16)
        self.rp_a = a("rp_a", [128, 512], F32)
        self.rp_b = a("rp_b", [128, 512], F32)
        self.mtc_sb = a("mtc_sb", [128, 4], BF16)
        self.mth_sb = a("mth_sb", [128, 4, 2], F32)
        self.nmb_sb = a("nmb_sb", [1, 4], BF16)
        self.gs_sb = a("gs_sb", [128, IC], F32)
        self.us_sb = a("us_sb", [128, IC], F32)
        self.dsraw_sb = a("dsraw_sb", [128, DC], F32)
        self.dsh_sb = a("dsh_sb", [128, DC], F32)
        self.mt_sb = a("mt_sb", [128, 16], F32)
        self.row_sb = a("row_sb", [16, 128], F32)
        self.diag_sb = a("diag_sb", [16, 16], BF16)
        self.negb_sb = a("negb_sb", [128, 16], F32)
        self.scr64a = a("scr64a", [64, 512], F32)
        self.scr64b = a("scr64b", [64, 512], F32)
        self.vfm_sb = a("vfm_sb", [128, 2, 512], BF16)

        self.arena_h = a("arena", [128, ARENA], mybir.dt.uint8)
        self.wb_h = a("wbregion", [128, WB], mybir.dt.uint8)
        self.arena_base = nc.lookup_mloc(self.arena_h).addr
        self.wb_base = nc.lookup_mloc(self.wb_h).addr

        pl = self.place
        ab = self.arena_base
        wb = self.wb_base
        self.ln = pl("ln_sl", [128, DC, TG], BF16, ab + 0)
        self.h = pl("h_sl", [128, IC, TG], BF16, ab + 32768)
        self.xin_fd = pl("xin_fd", [128, 4, 512], F32, ab + 145408)
        self.xout_fd = pl("xout_fd", [128, 4, 512], F32, ab + 153600)
        self.sqbf_l1 = pl("sqbf_l1", [128, DC, 512], BF16, ab + 122880)
        self.sqbf = pl("sqbf_sl", [128, 4, 512], BF16, ab + 139776)
        self.mn_scr = pl("mn_scr", [128, NU, 512], F32, ab + 49152)
        self.var_scr = pl("var_scr", [128, NU, 512], F32, ab + 53248)
        self.sd_scr = pl("sd_scr", [128, NU, 512], F32, ab + 57344)
        self.r_scr = pl("r_scr", [128, NU, 512], BF16, ab + 61440)
        self.mean_sb = pl("mean_sb", [128, NU, 512], BF16, ab + 63488)
        self.d_scr = pl("d_scr", [128, 4, 512], BF16, ab + 65536)
        self.sg_scr = pl("sg_scr", [128, 2, 512], F32, ab + 145408)
        self.ut_scr = pl("ut_scr", [128, 2, 512], F32, ab + 149504)
        self.qrot = pl("qrot_sl", [128, NH, TG], BF16, ab + 32768)
        self.kbuf = pl("kbuf_sl", [128, NH, WIN + TG], BF16, ab + 65536)
        self.vbuf = pl("vbuf_sl", [128, 10, D], BF16, ab + 106496)
        self.tbl_sb = pl("tbl_sl", [128, 4, TG], F32, ab + 147456)
        self.scores = pl("scores_sl", [128, 4, 1024], F32, ab + 0)
        self.psb = pl("p_sl", [128, 4, 1024], BF16, ab + 16384)
        self.rc = pl("rc_sl", [128, 4, WIN], F32, ab + 24576)
        self.rcscr = pl("rcscr_sl", [128, 4, WIN], F32, ab + 28672)
        self.ao_sb = self.qrot
        self.xin_fo = pl("xin_fo", [128, 4, 512], F32, ab + 65536)
        self.xout_fo = pl("xout_fo", [128, 4, 512], F32, ab + 73728)
        self.abuf = pl("abuf", [128, 6, DC, 128], BF16, wb + 0)
        self.dbuf_h = pl("dbuf_h", [128, 3, 22, 128], BF16, ab + 122880)

        ap_ = nc.alloc_psum_tensor
        self.pb = [ap_(f"pb{i}", [128, 512], F32) for i in range(6)]
        self.pb_pl = ap_("pb_pl", [128, 512], F32)
        self.pb_t = ap_("pb_t", [128, 1024], BF16)

        sem = nc.alloc_semaphore
        self.pe = Ctr(sem(name="s_pe"), "pe")
        self.dve = Ctr(sem(name="s_dve"), "dve")
        self.act = Ctr(sem(name="s_act"), "act")
        self.gd = Ctr(sem(name="s_gd"), "gd")
        self.xd = Ctr(sem(name="s_xd"), "xd")
        self.sd = Ctr(sem(name="s_sd"), "sd")
        self.sx = [Ctr(sem(name=f"s_sx{i}"), f"sx{i}") for i in range(2)]
        self.sa = [Ctr(sem(name=f"s_sa{i}"), f"sa{i}") for i in range(6)]
        self.sb = [Ctr(sem(name=f"s_sb{i}"), f"sb{i}") for i in range(3)]
        self.sxi = [Ctr(sem(name=f"s_sxi{i}"), f"sxi{i}") for i in range(4)]
        self.sso = [Ctr(sem(name=f"s_sso{i}"), f"sso{i}") for i in range(4)]
        self.smisc = [Ctr(sem(name=f"s_sm{i}"), f"sm{i}") for i in range(2)]
        self.gp = Ctr(sem(name="s_gp"), "gp")
        self.all_ctrs = ([self.pe, self.dve, self.act, self.gd, self.xd, self.sd,
                          self.gp]
                         + self.sx + self.sa + self.sb + self.sxi + self.sso
                         + self.smisc)
        self.eng_ctrs = {"tensor": (self.pe,), "vector": (self.dve,),
                         "scalar": (self.act,), "sync": (), "gpsimd": (self.gp,)}
        self.sqbf_war = {}
        self.sh_done = {}
        self._shadow_q = []
        self.a_ctr = 0
        self.a_war = {}
        self.b_ctr = 0
        self.b_war = {}
        self.b_extra_w = []
        self.lnrdy = {}
        self.ln_bankfree = {}

    def place(self, name, shape, dtype, addr):
        nc = self.nc
        h = nc.virtual_sbuf_tensor(name, shape, dtype)
        per_part = 1
        for s_ in shape[1:]:
            per_part *= s_
        per_part *= mybir.dt.size(dtype)
        nc._tensor(h.name, list(shape), dtype, type="SB")
        mloc = nc.lookup_mloc(h)
        mloc.allocated = True
        mloc.addr = addr
        return h

    def bar(self):
        self.p.barrier(self.eng_ctrs, self.all_ctrs)

    def soften(self, **eng_waits):
        for e, wl in eng_waits.items():
            self.p.pending[e] = list(wl)
            self.p.seen[e] = {}

    def _sso_totals(self):
        return [(c, c.n) for c in self.sso if c.n > 0]


    def prep(self):
        p = self.p
        loads = [
            (self.ident_sb, self.c_ident), (self.ones_sb, self.c_ones),
            (self.negones_sb, self.c_negones),
            (self.invd_sb, self.c_invd), (self.ones16_sb, self.c_ones16),
            (self.id16_sb, self.c_id16), (self.eps_sb, self.c_eps),
            (self.epsden_sb, self.c_epsden),
            (self.mask23_sb, self.c_mask23), (self.rotm_sb, self.c_rotm),
            (self.id128f_sb, self.c_id128f),
            (self.gs_sb, self.gsd), (self.us_sb, self.usd),
            (self.dsraw_sb, self.dsd),
        ]
        for dst, src in loads:
            p.emit("sync", lambda e, d=dst, s=src: e.dma_start(d[tuple(slice(None) for _ in d.shape)], s[tuple(slice(None) for _ in s.shape)]),
                   inc=self.xd, inc_n=16)
        v_xd = self.xd.n
        p.emit("vector", lambda e: e.tensor_scalar_mul(self.dsh_sb[:, :], self.dsraw_sb[:, :], 0.5),
               waits=[(self.xd, v_xd)], inc=self.dve)
        self.bar()

    def lnin(self, g, src=None):
        p = self.p
        if not hasattr(self, "sqbf_war"):
            self.sqbf_war = {}
        b, toff = g // 2, (g % 2) * TG
        for u in range(NU):
            if src is not None:
                pbm_key, pbq_key = ("pb", 2 * u), ("pb", 2 * u + 1)
                pbm = self.pb[2 * (u % 2)]
                pbq = self.pb[2 * (u % 2) + 1]
                sq_marks = []
                msq_marks = []
                mean_mark = None
                sl = slice(toff + u * 512, toff + u * 512 + 512)
                mld = p.emit("gpsimd", lambda e, sl=sl, u=u: e.dma_start(
                    self.ln[:, :, u * 512:(u + 1) * 512],
                    src[b, :, sl].rearrange("(c p) t -> p c t", p=128)),
                    inc=self.sx[u], inc_n=16)
                if not hasattr(self, "sqbf_l1_war"):
                    self.sqbf_l1_war = {}
                for c in range(DC):
                    sq_w = [(self.sx[u], mld)]
                    if c in self.sqbf_l1_war:
                        sq_w.append(self.sqbf_l1_war[c])
                    m = p.emit("scalar", lambda e, c=c, u=u: e.activation(
                        self.sqbf_l1[:, c, :], self.ln[:, c, u * 512:(u + 1) * 512],
                        AF.Square),
                        waits=sq_w, inc=self.act)
                    sq_marks.append(m)
                for c in range(DC):
                    mean_mark = p.emit("tensor", lambda e, c=c, u=u, pbm=pbm: e.matmul(
                        pbm[:, :], self.invd_sb[:, :],
                        self.ln[:, c, u * 512:(u + 1) * 512],
                        start=(c == 0), stop=(c == DC - 1)),
                        waits=[(self.sx[u], mld)],
                        inc=self.pe)
                for c in range(DC):
                    mm = p.emit("tensor", lambda e, c=c, u=u, pbq=pbq: e.matmul(
                        pbq[:, :], self.invd_sb[:, :], self.sqbf_l1[:, c, :],
                        start=(c == 0), stop=(c == DC - 1)),
                        waits=[(self.act, sq_marks[c])],
                        inc=self.pe)
                    msq_marks.append(mm)
                    self.sqbf_l1_war[c] = (self.pe, mm)
                w_mn = [(self.pe, mean_mark)]
                w_vr = [(self.pe, msq_marks[-1])]
            else:
                pbm_key, pbq_key = (("pb", 3), ("pb", 4)) if u == 0 else (("pb", 5), ("pl",))
                pbm = self.pb[3] if u == 0 else self.pb[5]
                pbq = self.pb[4] if u == 0 else self.pb_pl
                w_mn = []
                w_vr = []
            m_act = p.emit("scalar", lambda e, u=u, pbm=pbm: e.activation(
                self.mn_scr[:, u, :], pbm[:, :], AF.Square),
                waits=w_mn, inc=self.act)
            m_mev = p.emit("scalar", lambda e, u=u, pbm=pbm: e.activation(
                self.mean_sb[:, u, :], pbm[:, :], AF.Copy),
                inc=self.act)
            self.ln_bankfree[pbm_key] = (self.act, m_mev)
            m_dve = p.emit("vector", lambda e, u=u, pbq=pbq: e.tensor_tensor(
                self.var_scr[:, u, :], pbq[:, :], self.mn_scr[:, u, :], ALU.subtract),
                waits=w_vr + [(self.act, m_act)], inc=self.dve)
            self.ln_bankfree[pbq_key] = (self.dve, m_dve)
            m_act = p.emit("scalar", lambda e, u=u: e.activation(
                self.sd_scr[:, u, :], self.var_scr[:, u, :], AF.Ln, bias=self.eps_sb[:, :]),
                waits=[(self.dve, m_dve)], inc=self.act)
            m_act = p.emit("scalar", lambda e, u=u: e.activation(
                self.r_scr[:, u, :], self.sd_scr[:, u, :], AF.Exp, scale=-0.5),
                waits=[(self.act, m_act)], inc=self.act)
            sub_marks = {}
            mul_marks = {}

            def emit_mul(c, u=u):
                w = [(self.dve, sub_marks[c])]
                if c == 0:
                    w.append((self.act, m_act))
                ml = p.emit("vector", lambda e, c=c, u=u: e.tensor_tensor(
                    self.ln[:, c, u * 512:(u + 1) * 512], self.d_scr[:, c % 4, :],
                    self.r_scr[:, u, :], ALU.mult),
                    waits=w, inc=self.dve)
                mul_marks[c] = ml
                self.lnrdy[(u, c)] = (self.dve, ml)

            for c in range(DC):
                w = [(self.act, m_act)] if c == 0 else []
                if c >= 4:
                    w.append((self.dve, mul_marks[c - 4]))
                sub_marks[c] = p.emit("vector", lambda e, c=c, u=u: e.tensor_tensor(
                    self.d_scr[:, c % 4, :], self.ln[:, c, u * 512:(u + 1) * 512],
                    self.mean_sb[:, u, :], ALU.subtract),
                    waits=w, inc=self.dve)
                if c >= 1:
                    emit_mul(c - 1)
            emit_mul(DC - 1)
        self.bar()

    def _aload(self, src_ap, extra_w=()):
        p = self.p
        slot = self.a_ctr % 6
        self.a_ctr += 1
        w = list(extra_w)
        if slot in self.a_war:
            w.append(self.a_war.pop(slot))
        m = p.emit("gpsimd", lambda e, slot=slot, src_ap=src_ap: e.dma_start(
            self.abuf[:, slot, :, :],
            src_ap.rearrange("p (c i) -> p c i", c=DC)),
            waits=w, inc=self.sa[slot], inc_n=16)
        return slot, m

    def fgu(self, ln_entry=None):
        p = self.p
        self.hmul_done = {}
        ev_done = {}
        pend_h = []

        def flush_h(lag=0):
            while len(pend_h) > lag:
                j_, u_, ssl_, ma_, md_ = pend_h.pop(0)
                mh = p.emit("vector", lambda e, j=j_, u=u_, ssl=ssl_: e.tensor_tensor(
                    self.h[:, j, u * 512:(u + 1) * 512], self.sg_scr[:, ssl, :],
                    self.ut_scr[:, ssl, :], ALU.mult),
                    waits=[(self.act, ma_), (self.dve, md_)], inc=self.dve)
                self.hmul_done[ssl_] = mh
        for j in range(IC):
            slg, mg = self._aload(self.gw[j, :, :])
            slu, mu = self._aload(self.uw[j, :, :])
            for u in range(NU):
                par = (j * NU + u) % 2
                pbg = self.pb[par]
                pbu = self.pb[2 + par]
                w_g = [(self.sa[slg], mg)]
                w_u = [(self.sa[slu], mu)]
                if j == 0 and ln_entry is not None:
                    for (bk, wl) in ((("pb", par), w_g), (("pb", 2 + par), w_u)):
                        if bk in self.ln_bankfree:
                            wl.append(self.ln_bankfree[bk])
                        wl.extend(ln_entry)
                key_g = ("g", par)
                key_u = ("u", par)
                if key_g in ev_done:
                    w_g.append((self.act, ev_done[key_g]))
                if key_u in ev_done:
                    w_u.append((self.dve, ev_done[key_u]))
                for c in range(DC):
                    cw = list(w_g) if c == 0 else []
                    if j == 0 and ln_entry is not None:
                        cw.append(self.lnrdy[(u, c)])
                    p.emit("tensor", lambda e, slg=slg, u=u, c=c, pbg=pbg: e.matmul(
                        pbg[:, :], self.abuf[:, slg, c, :],
                        self.ln[:, c, u * 512:(u + 1) * 512],
                        start=(c == 0), stop=(c == DC - 1)),
                        waits=cw,
                        inc=self.pe if c == DC - 1 else None)
                mm_g = self.pe.n
                for c in range(DC):
                    cw = list(w_u) if c == 0 else []
                    if j == 0 and ln_entry is not None:
                        cw.append(self.lnrdy[(u, c)])
                    p.emit("tensor", lambda e, slu=slu, u=u, c=c, pbu=pbu: e.matmul(
                        pbu[:, :], self.abuf[:, slu, c, :],
                        self.ln[:, c, u * 512:(u + 1) * 512],
                        start=(c == 0), stop=(c == DC - 1)),
                        waits=cw,
                        inc=self.pe if c == DC - 1 else None)
                mm_u = self.pe.n
                if u == NU - 1:
                    self.a_war[slg] = (self.pe, mm_g)
                    self.a_war[slu] = (self.pe, mm_u)
                ssl = (j * NU + u) % 2
                sg_w = [(self.pe, mm_g)]
                if ssl in self.hmul_done:
                    sg_w.append((self.dve, self.hmul_done[ssl]))
                ma = p.emit("scalar", lambda e, j=j, ssl=ssl, pbg=pbg: e.activation(
                    self.sg_scr[:, ssl, :], pbg[:, :], AF.Silu, scale=self.gs_sb[:, j:j + 1]),
                    waits=sg_w, inc=self.act)
                ev_done[key_g] = ma
                md = p.emit("vector", lambda e, j=j, ssl=ssl, pbu=pbu: e.tensor_scalar(
                    self.ut_scr[:, ssl, :], pbu[:, :], self.us_sb[:, j:j + 1], None, ALU.mult),
                    waits=[(self.pe, mm_u)], inc=self.dve)
                ev_done[key_u] = md
                pend_h.append((j, u, ssl, ma, md))
                flush_h(1)
        flush_h(0)
        self.bar()

    def fdn(self, g, xsrc, xdst, shadow=False):
        p = self.p
        if not hasattr(self, "sqbf_war"):
            self.sqbf_war = {}
        b, toff = g // 2, (g % 2) * TG
        ev_done = {}
        stt_done = {}
        st_done = {}
        for j in range(DC):
            hm = []
            for half in range(2):
                slot = self.b_ctr % 3
                self.b_ctr += 1
                w = list(self.b_extra_w)
                self.b_extra_w = []
                if slot in self.b_war:
                    w.append(self.b_war.pop(slot))
                mw = p.emit("gpsimd", lambda e, j=j, slot=slot, half=half: e.dma_start(
                    self.dbuf_h[:, slot, :, :],
                    self.dw[j, :, half * 2816:(half + 1) * 2816]
                        .rearrange("p (c i) -> p c i", c=22)),
                    waits=w, inc=self.sb[slot], inc_n=16)
                hm.append((slot, mw))
            for u in range(NU):
                r = (j * NU + u) % 4
                sl = slice(toff + u * 512, toff + u * 512 + 512)
                ld_w = []
                if r in stt_done:
                    ld_w.append((self.dve, stt_done[r]))
                mx = p.emit("sync", lambda e, j=j, r=r, sl=sl: e.dma_start(
                    self.xin_fd[:, r, :], xsrc[b, j * 128:(j + 1) * 128, sl]),
                    waits=ld_w, inc=self.sxi[r], inc_n=16)
                par = (j * NU + u) % 3
                pbd = self.pb[par]
                w0 = [(self.sb[hm[0][0]], hm[0][1])]
                if par in ev_done:
                    w0.append((self.dve, ev_done[par]))
                m_halfa = None
                for c in range(IC):
                    half = 0 if c < 22 else 1
                    slot, mw = hm[half]
                    cw = w0 if c == 0 else (
                        [(self.sb[slot], mw)] if c == 22 else ())
                    want_inc = (c == IC - 1) or (c == 21 and u == NU - 1)
                    mk = p.emit("tensor", lambda e, slot=slot, u=u, c=c, half=half, pbd=pbd: e.matmul(
                        pbd[:, :], self.dbuf_h[:, slot, c - half * 22, :],
                        self.h[:, c, u * 512:(u + 1) * 512],
                        start=(c == 0), stop=(c == IC - 1)),
                        waits=cw,
                        inc=self.pe if want_inc else None)
                    if c == 21 and u == NU - 1:
                        m_halfa = mk
                mm = self.pe.n
                if u == NU - 1:
                    self.b_war[hm[0][0]] = (self.pe, m_halfa)
                    self.b_war[hm[1][0]] = (self.pe, mm)
                stt_w = [(self.pe, mm), (self.sxi[r], mx)]
                if r in st_done:
                    stt_w.append((self.sso[r], st_done[r]))
                if r in self.sh_done:
                    stt_w.append(self.sh_done.pop(r))
                me = p.emit("vector", lambda e, j=j, r=r, pbd=pbd: e.scalar_tensor_tensor(
                    self.xout_fd[:, r, :], pbd[:, :], self.dsh_sb[:, j:j + 1],
                    self.xin_fd[:, r, :], ALU.mult, ALU.add),
                    waits=stt_w, inc=self.dve)
                ev_done[par] = me
                stt_done[r] = me
                st_done[r] = p.emit("scalar", lambda e, j=j, r=r, sl=sl: e.dma_start(
                    xdst[b, j * 128:(j + 1) * 128, sl], self.xout_fd[:, r, :]),
                    waits=[(self.dve, me)], inc=self.sso[r], inc_n=16)
                if shadow:
                    self.sh_done[r] = self._shadow_unit(j, u, r, self.xout_fd, me)
                    self._shadow_flush(2)
        self._shadow_flush(0)
        self.bar()

    def _shadow_unit(self, j, u, r, xout, me):
        p = self.p
        slot = (j * NU + u) % 4
        msh = p.emit("scalar", lambda e, j=j, u=u, r=r: e.activation(
            self.ln[:, j, u * 512:(u + 1) * 512], xout[:, r, :], AF.Copy),
            waits=[(self.dve, me)], inc=self.act)
        sq_w = [(self.act, msh)]
        if slot in self.sqbf_war:
            sq_w.append(self.sqbf_war[slot])
        for c_ in self.smisc:
            if c_.n > 0:
                sq_w.append((c_, c_.n))
        msq_a = p.emit("scalar", lambda e, j=j, u=u, slot=slot: e.activation(
            self.sqbf[:, slot, :], self.ln[:, j, u * 512:(u + 1) * 512], AF.Square),
            waits=sq_w, inc=self.act)
        self._shadow_q.append((j, u, slot, msh, msq_a))
        return (self.act, msh)

    def _shadow_mms(self, j, u, slot, msh, msq_a):
        p = self.p
        pbm = self.pb[3] if u == 0 else self.pb[5]
        pbq = self.pb[4] if u == 0 else self.pb_pl
        p.emit("tensor", lambda e, j=j, u=u, pbm=pbm: e.matmul(
            pbm[:, :], self.invd_sb[:, :], self.ln[:, j, u * 512:(u + 1) * 512],
            start=(j == 0), stop=(j == DC - 1), skip_group_check=True),
            waits=[(self.act, msh)], inc=self.pe)
        mq = p.emit("tensor", lambda e, j=j, slot=slot, pbq=pbq: e.matmul(
            pbq[:, :], self.invd_sb[:, :], self.sqbf[:, slot, :],
            start=(j == 0), stop=(j == DC - 1), skip_group_check=True),
            waits=[(self.act, msq_a)], inc=self.pe)
        self.sqbf_war[slot] = (self.pe, mq)

    def _shadow_flush(self, lag):
        while len(self._shadow_q) > lag:
            self._shadow_mms(*self._shadow_q.pop(0))


    def qkv(self, g, ln_entry=None):
        self.p.cur_label = "qkv"
        p = self.p
        self._qkv_ln_entry = ln_entry
        b, toff = g // 2, (g % 2) * TG
        batch_start = (g % 2 == 0)
        mtbl = p.emit("sync", lambda e: e.dma_start(
            self.tbl_sb[:, :, :], self.c_tbl[:, :, toff:toff + TG]),
            inc=self.xd, inc_n=16)
        if not batch_start:
            p.emit("sync", lambda e: e.dma_start(
                self.kbuf[:, :, 0:WIN], self.kprev_d[:, :, :]),
                waits=[(self.smisc[0], self.smisc[0].n), (self.smisc[1], self.smisc[1].n),
                       (self.pe, self.pe.n), (self.dve, self.dve.n), (self.act, self.act.n)],
                inc=self.xd, inc_n=16)
            p.emit("sync", lambda e: e.dma_start(
                self.vbuf[:, 0:2, :], self.vprev_d[:, :, :]),
                inc=self.xd, inc_n=16)

        slabs = ([("q", j) for j in range(DC)] + [("k", j) for j in range(DC)]
                 + [("v", j) for j in range(DC)])
        bank_free = {}
        rot_free = {}
        rp_free = [None]
        qtmp_free = {}
        vfm_free = {}
        pbt_free = {}
        srcs = {"q": self.wq, "k": self.wk, "v": self.wv}
        rotbanks = [self.pb[4], self.pb[5], self.pb_pl]
        deferred = []
        unit = 0

        def qk_bundle(kind, j, u, tslot, rslot, mc):
            def go():
                ti = 2 if kind == "q" else 0
                pbr = rotbanks[rslot]
                w_r = [(self.act, mc)]
                if rslot in rot_free:
                    w_r.append(rot_free[rslot])
                elif self._qkv_ln_entry is not None:
                    bk = [("pb", 4), ("pb", 5), ("pl",)][rslot]
                    if bk in self.ln_bankfree:
                        w_r.append(self.ln_bankfree[bk])
                    w_r.extend(self._qkv_ln_entry)
                mr = p.emit("tensor", lambda e, tslot=tslot, pbr=pbr: e.matmul(
                    pbr[:, :], self.rotm_sb[:, :], self.qtmp_sb[:, tslot, :],
                    start=True, stop=True),
                    waits=w_r, inc=self.pe)
                w_a1 = [(self.act, mc), (self.xd, mtbl)]
                if rp_free[0] is not None:
                    w_a1.append(rp_free[0])
                ma1 = p.emit("vector", lambda e, tslot=tslot, ti=ti, u=u: e.tensor_mul(
                    self.rp_a[:, :], self.qtmp_sb[:, tslot, :],
                    self.tbl_sb[:, ti, u * 512:(u + 1) * 512]),
                    waits=w_a1, inc=self.dve)
                ma2 = p.emit("vector", lambda e, pbr=pbr, ti=ti, u=u: e.tensor_mul(
                    self.rp_b[:, :], pbr[:, :],
                    self.tbl_sb[:, ti + 1, u * 512:(u + 1) * 512]),
                    waits=[(self.pe, mr)], inc=self.dve)
                rot_free[rslot] = (self.dve, ma2)
                if kind == "q":
                    dst = self.qrot[:, j, u * 512:(u + 1) * 512]
                else:
                    dst = self.kbuf[:, j, WIN + u * 512: WIN + (u + 1) * 512]
                madd = p.emit("vector", lambda e, dst=dst: e.tensor_add(
                    dst, self.rp_a[:, :], self.rp_b[:, :]),
                    waits=[(self.dve, ma2)], inc=self.dve)
                rp_free[0] = (self.dve, madd)
                qtmp_free[tslot] = [(self.dve, ma1), (self.pe, mr)]
            return go

        def v_bundle(j, u, vslot, ph, mc):
            def go():
                w_t = [(self.act, mc)]
                if ph in pbt_free:
                    w_t.append(pbt_free[ph])
                mtt = None
                for tc in range(4):
                    mtt = p.emit("tensor", lambda e, ph=ph, tc=tc, vslot=vslot: e.transpose(
                        self.pb_t[:, ph * 512 + tc * 128: ph * 512 + (tc + 1) * 128],
                        self.vfm_sb[:, vslot, tc * 128:(tc + 1) * 128],
                        self.ident_sb[:, :]),
                        waits=w_t if tc == 0 else (),
                        inc=self.pe)
                vfm_free[vslot] = (self.pe, mtt)
                mdv = p.emit("vector", lambda e, ph=ph, u=u, j=j: e.tensor_copy(
                    self.vbuf[:, 2 + u * 4: 2 + u * 4 + 4, j * 128:(j + 1) * 128],
                    bass.AP(self.pb_t, ph * 512, [[1024, 128], [128, 4], [1, 128]])),
                    waits=[(self.pe, mtt)], inc=self.dve)
                pbt_free[ph] = (self.dve, mdv)
            return go

        banks_seen = set()
        for s_idx, (kind, j) in enumerate(slabs):
            slot, msl = self._aload(srcs[kind][j, :, :])
            for u in range(NU):
                while deferred and deferred[0][0] <= unit - 2:
                    deferred.pop(0)[1]()
                bank = (2 * s_idx + u) % 4
                pb = self.pb[bank]
                w0 = [(self.sa[slot], msl)]
                if bank in bank_free:
                    w0.append(bank_free[bank])
                elif ln_entry is not None and bank not in banks_seen:
                    bk = ("pb", bank)
                    if bk in self.ln_bankfree:
                        w0.append(self.ln_bankfree[bk])
                    w0.extend(ln_entry)
                banks_seen.add(bank)
                for c in range(DC):
                    cw = list(w0) if c == 0 else []
                    if s_idx == 0 and ln_entry is not None:
                        cw.append(self.lnrdy[(u, c)])
                    mmark = p.emit("tensor", lambda e, slot=slot, c=c, u=u, pb=pb: e.matmul(
                        pb[:, :], self.abuf[:, slot, c, :],
                        self.ln[:, c, u * 512:(u + 1) * 512],
                        start=(c == 0), stop=(c == DC - 1)),
                        waits=cw,
                        inc=self.pe)
                if u == NU - 1:
                    self.a_war[slot] = (self.pe, mmark)
                if kind in ("q", "k"):
                    tslot = unit % 4
                    w_c = [(self.pe, mmark)]
                    if tslot in qtmp_free:
                        w_c.extend(qtmp_free[tslot])
                    mc = p.emit("scalar", lambda e, tslot=tslot, pb=pb: e.activation(
                        self.qtmp_sb[:, tslot, :], pb[:, :], AF.Copy),
                        waits=w_c, inc=self.act)
                    bank_free[bank] = (self.act, mc)
                    deferred.append((unit, qk_bundle(kind, j, u, tslot, unit % 3, mc)))
                else:
                    vslot = unit % 2
                    w_c = [(self.pe, mmark)]
                    if vslot in vfm_free:
                        w_c.append(vfm_free[vslot])
                    mc = p.emit("scalar", lambda e, vslot=vslot, pb=pb: e.activation(
                        self.vfm_sb[:, vslot, :], pb[:, :], AF.Copy),
                        waits=w_c, inc=self.act)
                    bank_free[bank] = (self.act, mc)
                    deferred.append((unit, v_bundle(j, u, vslot, unit % 2, mc)))
                unit += 1
        while deferred:
            deferred.pop(0)[1]()
        self.bar()

    def core(self, g):
        self.p.cur_label = "core"
        p = self.p
        NWH = 4 * NH
        marks = {}

        def geom(widx):
            w, h = widx // NH, widx % NH
            wa = (g % 2) * 4 + w
            return w, h, wa > 0

        def emit_exp(widx):
            _, _, has_prev = geom(widx)
            nch = 4 if has_prev else 2
            sl = widx % 4
            w_e = [marks[("negb", widx)]] + list(marks.get(("psb_slot", sl), []))
            if nch == 4:
                w_e.append(marks[("ev_prev", widx)])
            mex = p.emit("scalar", lambda e, sl=sl, nch=nch: e.activation(
                self.psb[:, sl, 0:nch * 256], self.scores[:, sl, 0:nch * 256],
                AF.Exp, bias=self.negb_sb[:, sl:sl + 1]),
                waits=w_e, inc=self.act)
            marks[("scores_slot", sl)] = [(self.act, mex)]
            marks[("exp", widx)] = (self.act, mex)

        def emit_scores(widx):
            w, h, has_prev = geom(widx)
            nch = 4 if has_prev else 2
            npair = nch // 2
            sl = widx % 4
            bset = 4 if (widx % 2 == 0) else 2
            for pr in range(npair):
                pbs = self.pb[bset + pr]
                w0 = list(marks.get(("scfree", bset + pr), []))
                for kk in range(2):
                    kc = pr * 2 + kk
                    kcol = w * 256 + (0 if has_prev else 256) + kc * 128
                    mmark = p.emit("tensor", lambda e, pbs=pbs, kk=kk, h=h, kcol=kcol, w=w: e.matmul(
                        pbs[:, kk * 256:(kk + 1) * 256],
                        self.kbuf[:, h, kcol:kcol + 128],
                        self.qrot[:, h, w * 256:(w + 1) * 256],
                        start=(kk == 0), stop=(kk == 1), skip_group_check=True),
                        waits=w0 if kk == 0 else (),
                        inc=self.pe)
                marks[("scmm", widx, pr)] = (self.pe, mmark)

        def emit_gmax(widx):
            _, _, has_prev = geom(widx)
            npair = 2 if has_prev else 1
            sl = widx % 4
            w_n = [marks[("evac_dve", widx)]] + list(marks.get(("nmb_slot", sl), []))
            mnm = p.emit("gpsimd", lambda e, sl=sl, npair=npair: e.tensor_reduce(
                self.nmb_sb[0:1, sl:sl + 1],
                self.scores[:, sl, (npair - 1) * 512:npair * 512],
                AX.XYZWC, ALU.max),
                waits=w_n, inc=self.gp)
            marks[("nm", widx)] = (self.gp, mnm)

        def emit_bcast(widx):
            sl = widx % 4
            w_b = [marks[("nm", widx)]] + list(marks.get(("pbpl", sl), []))
            mbc = p.emit("tensor", lambda e, sl=sl: e.matmul(
                self.pb_pl[:, sl:sl + 1], self.negones_sb[0:1, :],
                self.nmb_sb[0:1, sl:sl + 1],
                start=True, stop=True, skip_group_check=True),
                waits=w_b, inc=self.pe)
            marks[("nmb_slot", sl)] = [(self.pe, mbc)]
            marks[("bc", widx)] = (self.pe, mbc)

        def emit_negb(widx):
            sl = widx % 4
            mng = p.emit("scalar", lambda e, sl=sl: e.activation(
                self.negb_sb[:, sl:sl + 1], self.pb_pl[:, sl:sl + 1], AF.Copy),
                waits=[marks[("bc", widx)]], inc=self.act)
            marks[("pbpl", sl)] = [(self.act, mng)]
            marks[("negb", widx)] = (self.act, mng)

        def emit_evac(widx):
            _, _, has_prev = geom(widx)
            nch = 4 if has_prev else 2
            npair = nch // 2
            sl = widx % 4
            bset = 4 if (widx % 2 == 0) else 2
            for pr in range(npair):
                pbs = self.pb[bset + pr]
                wv = [marks[("scmm", widx, pr)]] + list(marks.get(("scores_slot", sl), []))
                if pr == npair - 1:
                    me = p.emit("vector", lambda e, pbs=pbs, sl=sl, pr=pr: e.tensor_add(
                        self.scores[:, sl, pr * 512:(pr + 1) * 512], pbs[:, :],
                        bass.AP(self.mask23_sb, 0, [[512, 128], [1, 512]])),
                        waits=wv, inc=self.dve)
                    marks[("scfree", bset + pr)] = [(self.dve, me)]
                    marks[("evac_dve", widx)] = (self.dve, me)
                else:
                    me = p.emit("vector", lambda e, pbs=pbs, sl=sl, pr=pr: e.tensor_copy(
                        self.scores[:, sl, pr * 512:(pr + 1) * 512], pbs[:, :]),
                        waits=wv, inc=self.dve)
                    marks[("scfree", bset + pr)] = [(self.dve, me)]
                    marks[("ev_prev", widx)] = (self.dve, me)

        def emit_denpv(widx):
            w, h, has_prev = geom(widx)
            nch = 4 if has_prev else 2
            sl = widx % 4
            hb = widx % 2
            pbb = self.pb[hb]
            w_d = [marks[("exp", widx)]] + list(marks.get(("denout", hb), []))
            for kc in range(nch):
                p.emit("tensor", lambda e, pbb=pbb, sl=sl, kc=kc, nch=nch: e.matmul(
                    pbb[:, 0:256], self.ones_sb[:, :],
                    self.psb[:, sl, kc * 256:(kc + 1) * 256],
                    start=(kc == 0), stop=(kc == nch - 1), skip_group_check=True),
                    waits=w_d if kc == 0 else (),
                    inc=self.pe)
            for kc in range(nch):
                vidx = (2 * w + kc) if has_prev else (2 * w + 2 + kc)
                mpv = p.emit("tensor", lambda e, pbb=pbb, sl=sl, kc=kc, vidx=vidx, h=h, nch=nch: e.matmul(
                    pbb[:, 256:512], self.vbuf[:, vidx, h * 128:(h + 1) * 128],
                    self.psb[:, sl, kc * 256:(kc + 1) * 256],
                    start=False, stop=(kc == nch - 1), skip_group_check=True),
                    inc=self.pe)
            marks[("psb_slot", sl)] = [(self.pe, mpv)]
            marks[("denpv", widx)] = (self.pe, mpv)

        def emit_rcscr(widx):
            sl = widx % 4
            hb = widx % 2
            pbb = self.pb[hb]
            w_r = [marks[("denpv", widx)]] + list(marks.get(("rcln_slot", sl), []))
            mr1 = p.emit("scalar", lambda e, pbb=pbb, sl=sl: e.activation(
                self.rcscr[:, sl, :], pbb[:, 0:256], AF.Ln, bias=self.epsden_sb[:, :]),
                waits=w_r, inc=self.act)
            marks[("rcscr", widx)] = (self.act, mr1)

        def emit_rc_ao(widx):
            w, h, has_prev = geom(widx)
            npair = 2 if has_prev else 1
            sl = widx % 4
            hb = widx % 2
            pbb = self.pb[hb]
            w_r2 = [marks[("rcscr", widx)]] + list(marks.get(("rc_slot", sl), []))
            mr2 = p.emit("scalar", lambda e, sl=sl: e.activation(
                self.rc[:, sl, :], self.rcscr[:, sl, :], AF.Exp, scale=-1.0),
                waits=w_r2, inc=self.act)
            marks[("rcln_slot", sl)] = [(self.act, mr2)]
            w_a = [(self.act, mr2), marks[("scmm", widx, npair - 1)]]
            mao = p.emit("vector", lambda e, h=h, w=w, sl=sl, pbb=pbb: e.tensor_mul(
                self.qrot[:, h, w * 256:(w + 1) * 256], pbb[:, 256:512],
                self.rc[:, sl, :]),
                waits=w_a, inc=self.dve)
            marks[("rc_slot", sl)] = [(self.dve, mao)]
            marks[("denout", hb)] = [(self.act, marks[("rcscr", widx)][1]), (self.dve, mao)]
            marks[("ao", widx)] = (self.dve, mao)

        for t in range(NWH + 5):
            if 0 <= t - 2 < NWH:
                emit_exp(t - 2)
            if t < NWH:
                emit_scores(t)
            if 0 <= t - 1 < NWH:
                emit_bcast(t - 1)
            if 0 <= t - 4 < NWH:
                emit_rc_ao(t - 4)
            if 0 <= t - 1 < NWH:
                emit_negb(t - 1)
            if t < NWH:
                emit_evac(t)
                emit_gmax(t)
            if 0 <= t - 3 < NWH:
                emit_denpv(t - 3)
                emit_rcscr(t - 3)
        self.bar()

    def oproj_preload(self, qkv_end):
        self.wo_pre = {}
        for j in range(6):
            self.wo_pre[j] = self._aload(self.wo[j, :, :],
                                         extra_w=[(self.pe, qkv_end)])

    def oproj(self, g, xsrc, xdst, shadow=False):
        p = self.p
        b, toff = g // 2, (g % 2) * TG
        if g % 2 == 0:
            p.emit("sync", lambda e: e.dma_start(
                self.kprev_d[:, :, :], self.kbuf[:, :, TG:TG + WIN]),
                inc=self.smisc[0], inc_n=16)
            p.emit("sync", lambda e: e.dma_start(
                self.vprev_d[:, :, :], self.vbuf[:, 8:10, :]),
                inc=self.smisc[1], inc_n=16)
        bank_free = {}
        xin_done = {}
        st_done = {}
        for j in range(DC):
            if hasattr(self, "wo_pre") and j in self.wo_pre:
                slot, msl = self.wo_pre.pop(j)
            else:
                slot, msl = self._aload(self.wo[j, :, :])
            for u in range(NU):
                r = (j * NU + u) % 4
                sl = slice(toff + u * 512, toff + u * 512 + 512)
                ld_w = []
                if r in xin_done:
                    ld_w.append((self.dve, xin_done[r]))
                mx = p.emit("sync", lambda e, j=j, r=r, sl=sl: e.dma_start(
                    self.xin_fo[:, r, :], xsrc[b, j * 128:(j + 1) * 128, sl]),
                    waits=ld_w, inc=self.sxi[r], inc_n=16)
                bank = (j * NU + u) % (3 if shadow else 4)
                pb = self.pb[bank]
                w0 = [(self.sa[slot], msl)]
                if bank in bank_free:
                    w0.append(bank_free[bank])
                for c in range(DC):
                    mmark = p.emit("tensor", lambda e, slot=slot, c=c, u=u, pb=pb: e.matmul(
                        pb[:, :], self.abuf[:, slot, c, :],
                        self.ao_sb[:, c, u * 512:(u + 1) * 512],
                        start=(c == 0), stop=(c == DC - 1)),
                        waits=w0 if c == 0 else (),
                        inc=self.pe)
                if u == NU - 1:
                    self.a_war[slot] = (self.pe, mmark)
                stt_w = [(self.pe, mmark), (self.sxi[r], mx)]
                if r in st_done:
                    stt_w.append((self.sso[r], st_done[r]))
                if r in self.sh_done:
                    stt_w.append(self.sh_done.pop(r))
                me = p.emit("vector", lambda e, r=r, pb=pb: e.tensor_add(
                    self.xout_fo[:, r, :], pb[:, :], self.xin_fo[:, r, :]),
                    waits=stt_w, inc=self.dve)
                bank_free[bank] = (self.dve, me)
                xin_done[r] = me
                st_done[r] = p.emit("scalar", lambda e, j=j, r=r, sl=sl: e.dma_start(
                    xdst[b, j * 128:(j + 1) * 128, sl], self.xout_fo[:, r, :]),
                    waits=[(self.dve, me)], inc=self.sso[r], inc_n=16)
                if shadow:
                    self.sh_done[r] = self._shadow_unit(j, u, r, self.xout_fo, me)
                    self._shadow_flush(2)
        self._shadow_flush(0)
        self.bar()


    def build(self):
        self.prep()
        pe = self.pe
        prev_dn_end = 0
        g2_end_prev = 0
        for g in range(NG):
            self.soften(gpsimd=[(pe, g2_end_prev)])
            self.lnin(g, self.xin)
            l1_end = pe.n
            self.soften(gpsimd=[], tensor=[])
            self.fgu(ln_entry=[])
            fgu_end = pe.n
            if self.stage == 1:
                self.b_extra_w = [(pe, l1_end)]
                self.fdn(g, self.xin, self.y)
                continue
            self.soften(sync=[(pe, fgu_end), (self.dve, self.dve.n),
                              (self.act, self.act.n)])
            self.b_extra_w = [(pe, l1_end)]
            self.fdn(g, self.xin, self.x1, shadow=True)
            dn1_end = pe.n
            dn1_dve = self.dve.n
            self.lnin(g)
            self.soften(gpsimd=[], tensor=[],
                        sync=[(pe, dn1_end), (self.dve, self.dve.n)] + self._sso_totals())
            self.qkv(g, ln_entry=[(self.dve, dn1_dve)])
            qkv_end = pe.n
            self.oproj_preload(qkv_end)
            self.soften(sync=[])
            self.core(g)
            core_end = pe.n
            self.soften(gpsimd=[(pe, qkv_end)],
                        sync=[(pe, core_end)] + self._sso_totals())
            if self.stage == 2:
                self.oproj(g, self.x1, self.y)
                continue
            self.oproj(g, self.x1, self.x2, shadow=True)
            op_end = pe.n
            op_dve = self.dve.n
            self.lnin(g)
            self.soften(gpsimd=[], tensor=[])
            self.fgu(ln_entry=[(self.dve, op_dve)])
            g2_end_prev = pe.n
            self.soften(sync=[(pe, pe.n), (self.dve, self.dve.n),
                              (self.act, self.act.n)] + self._sso_totals())
            self.b_extra_w = ([(pe, op_end)]
                              + [(c, c.n) for c in self.smisc if c.n > 0])
            self.fdn(g, self.x2, self.y)
            prev_dn_end = pe.n
        fin = [(c, c.n) for c in self.sso if c.n > 0]
        if self.sd.n > 0:
            fin.append((self.sd, self.sd.n))
        self.p.emit("scalar", None, waits=fin)
        build_block(self.nc, self.p)



def _prep_weights(inputs):
    gw = inputs["L0_ffn_gate_w"]
    uw = inputs["L0_ffn_up_w"]
    dw = inputs["L0_ffn_down_w"]
    arr = {}
    arr["gw"] = np.ascontiguousarray(gw.reshape(IC, 128, DC, 128).transpose(0, 3, 2, 1).reshape(IC, 128, DC * 128))
    arr["uw"] = np.ascontiguousarray(uw.reshape(IC, 128, DC, 128).transpose(0, 3, 2, 1).reshape(IC, 128, DC * 128))
    arr["dw"] = np.ascontiguousarray(dw.reshape(DC, 128, IC, 128).transpose(0, 3, 2, 1).reshape(DC, 128, IC * 128))
    arr["gsd"] = np.ascontiguousarray(inputs["L0_ffn_gate_s"].reshape(IC, 128).T)
    arr["usd"] = np.ascontiguousarray(inputs["L0_ffn_up_s"].reshape(IC, 128).T)
    arr["dsd"] = np.ascontiguousarray(inputs["L0_ffn_down_s"].reshape(DC, 128).T)
    for nm, w in [("wqt", inputs["L0_attn_Wq"]), ("wkt", inputs["L0_attn_Wk"]),
                  ("wvt", inputs["L0_attn_Wv"]), ("wot", inputs["L0_attn_Wo"])]:
        arr[nm] = np.ascontiguousarray(
            np.asarray(w).reshape(DC, 128, DC, 128).transpose(2, 1, 0, 3).reshape(DC, 128, DC * 128))
    return arr


def kernel(**inputs):
    hs = np.asarray(inputs["L0_attn_hidden_states"])
    warr = _prep_weights({k: np.asarray(v) for k, v in inputs.items()})

    nc = bass.Bass("TRN2", target_bir_lowering=False)
    mk = MK(nc)
    mk.build()

    in_maps = []
    for core in range(N_CORES):
        xin_c = np.ascontiguousarray(hs[2 * core:2 * core + 2].transpose(0, 2, 1))
        m = {"xin": xin_c}
        m.update(warr)
        in_maps.append(m)

    res = run_bass_kernel_spmd(nc, in_maps, core_ids=list(range(N_CORES)))
    out = np.empty((16, S, D), dtype=np.float32)
    for core in range(N_CORES):
        out[2 * core:2 * core + 2] = res.results[core]["y"].transpose(0, 2, 1)
    return out


if __name__ == "__main__":
    pass


# version 17
# speedup vs baseline: 1.1220x; 1.0724x over previous
import math
import numpy as np
import ml_dtypes
import concourse.bass as bass
import concourse.mybir as mybir
from concourse.bass_utils import run_bass_kernel_spmd

F32 = mybir.dt.float32
BF16 = mybir.dt.bfloat16
AF = mybir.ActivationFunctionType
ALU = mybir.AluOpType
AX = mybir.AxisListType

D = 2048
DC = 16
I_DIM = 5632
IC = 44
S = 2048
B_CORE = 2
TG = 1024
NG = 4
NU = 2
WIN = 256
NH = 16
HD = 128
NEG = -1e30
N_CORES = 8

STAGE = 3


class Ctr:
    __slots__ = ("h", "n", "name")
    def __init__(self, h, name):
        self.h = h
        self.n = 0
        self.name = name

class Prog:
    ENGS = ("tensor", "vector", "scalar", "sync", "gpsimd")

    def __init__(self):
        self.ops = {e: [] for e in self.ENGS}
        self.pending = {e: [] for e in self.ENGS}
        self.seen = {e: {} for e in self.ENGS}
        self.since_barrier = {e: None for e in self.ENGS}

    def emit(self, eng, fn, waits=(), inc=None, inc_n=1):
        wl = self.pending[eng]
        self.pending[eng] = []
        for (c, v) in waits:
            wl.append((c, v))
        out_waits = []
        for (c, v) in wl:
            prev = self.seen[eng].get(c.h.num, -1)
            if v > prev:
                self.seen[eng][c.h.num] = v
                out_waits.append((c.h, v))
        incs = []
        ret = None
        if inc is not None:
            inc.n += inc_n
            incs.append((inc.h, inc_n))
            ret = inc.n
        self.ops[eng].append([fn, out_waits, incs])
        self.since_barrier[eng] = len(self.ops[eng]) - 1
        return ret

    def ensure_inc(self, eng, ctr):
        idx = self.since_barrier[eng]
        if idx is None:
            return
        rec = self.ops[eng][idx]
        for (h, n) in rec[2]:
            if h.num == ctr.h.num:
                return
        ctr.n += 1
        rec[2].append((ctr.h, 1))

    def barrier(self, eng_ctrs, all_ctrs):
        snap = [(c, c.n) for c in all_ctrs if c.n > 0]
        for e in self.ENGS:
            own = eng_ctrs.get(e, ())
            own_nums = {c.h.num for c in own}
            for (c, v) in snap:
                if c.h.num not in own_nums:
                    self.pending[e].append((c, v))
            self.since_barrier[e] = None


def build_block(nc, prog):
    with nc.Block() as block:
        for eng_name in Prog.ENGS:
            ops = prog.ops[eng_name]
            def runner(eng, ops=ops):
                for fn, waits, incs in ops:
                    for (h, v) in waits:
                        eng.wait_ge(h, v)
                    ins = fn(eng) if fn is not None else None
                    if incs:
                        assert ins is not None
                        for (h, n) in incs:
                            ins.then_inc(h, n)
            getattr(block, eng_name)(runner)



def make_tables():
    inv_freq = (10000.0 ** (-(np.arange(0, 64, dtype=np.float32)) / 64.0)).astype(np.float32)
    t = np.arange(S, dtype=np.float32)
    ang = (t[:, None] * inv_freq[None, :]).astype(np.float32)
    cos = np.cos(ang).astype(np.float32).T
    sin = np.sin(ang).astype(np.float32).T
    sc = np.float32(1.0 / math.sqrt(HD))
    cosf = np.concatenate([cos, cos], axis=0)
    sinf = np.concatenate([sin, sin], axis=0)
    tbl = np.stack([cosf, sinf, cosf * sc, sinf * sc], axis=1)
    j = np.arange(256)[:, None]
    q = np.arange(256)[None, :]
    mask23 = np.where(j <= q, 0.0, NEG).astype(np.float32)
    m = np.zeros((128, 2, 256), dtype=np.float32)
    m[:, 0, :] = mask23[0:128, :]
    m[:, 1, :] = mask23[128:256, :]
    rotm = np.zeros((128, 128), dtype=ml_dtypes.bfloat16)
    for d in range(64):
        rotm[d + 64, d] = -1.0
    for d in range(64, 128):
        rotm[d - 64, d] = 1.0
    return np.ascontiguousarray(tbl), np.ascontiguousarray(m), rotm



ARENA = 163840
WB = 24576

class MK:
    def __init__(self, nc, stage=STAGE):
        self.nc = nc
        self.stage = stage
        self.p = Prog()

        dt = nc.dram_tensor
        self.xin = dt("xin", [B_CORE, D, S], F32, kind="ExternalInput").ap()
        self.gw = dt("gw", [IC, 128, DC * 128], F32, kind="ExternalInput").ap()
        self.uw = dt("uw", [IC, 128, DC * 128], F32, kind="ExternalInput").ap()
        self.dw = dt("dw", [DC, 128, IC * 128], F32, kind="ExternalInput").ap()
        self.gsd = dt("gsd", [128, IC], F32, kind="ExternalInput").ap()
        self.usd = dt("usd", [128, IC], F32, kind="ExternalInput").ap()
        self.dsd = dt("dsd", [128, DC], F32, kind="ExternalInput").ap()
        self.wq = dt("wqt", [DC, 128, DC * 128], F32, kind="ExternalInput").ap()
        self.wk = dt("wkt", [DC, 128, DC * 128], F32, kind="ExternalInput").ap()
        self.wv = dt("wvt", [DC, 128, DC * 128], F32, kind="ExternalInput").ap()
        self.wo = dt("wot", [DC, 128, DC * 128], F32, kind="ExternalInput").ap()
        self.y = dt("y", [B_CORE, D, S], F32, kind="ExternalOutput").ap()
        self.x1 = dt("x1", [B_CORE, D, S], F32).ap()
        self.x2 = dt("x2", [B_CORE, D, S], F32).ap()
        self.kprev_d = dt("kprev_d", [128, NH, WIN], BF16).ap()
        self.vprev_d = dt("vprev_d", [128, 2, D], BF16).ap()

        tbl, mask23, rotm = make_tables()
        it = nc.inline_tensor
        self.c_tbl = it(tbl, name="c_tbl").ap()
        self.c_mask23 = it(mask23, name="c_mask23").ap()
        self.c_rotm = it(rotm, name="c_rotm").ap()
        self.c_id128f = it(np.eye(128, dtype=np.float32), name="c_id128f").ap()
        self.c_ident = it(np.eye(128, dtype=ml_dtypes.bfloat16), name="c_ident").ap()
        self.c_ones = it(np.ones((128, 128), dtype=ml_dtypes.bfloat16), name="c_ones").ap()
        self.c_negones = it(np.full((1, 128), -1.0, dtype=ml_dtypes.bfloat16), name="c_negones").ap()
        self.c_invd = it(np.full((128, 128), 1.0 / D, dtype=ml_dtypes.bfloat16), name="c_invd").ap()
        self.c_ones16 = it(np.ones((16, 128), dtype=ml_dtypes.bfloat16), name="c_ones16").ap()
        self.c_id16 = it(np.eye(16, dtype=np.float32), name="c_id16").ap()
        self.c_eps = it(np.full((128, 1), 1e-5, dtype=np.float32), name="c_eps").ap()
        self.c_epsden = it(np.full((128, 1), 1e-20, dtype=np.float32), name="c_epsden").ap()

        a = nc.alloc_sbuf_tensor
        self.ident_sb = a("ident_sb", [128, 128], BF16)
        self.ones_sb = a("ones_sb", [128, 128], BF16)
        self.negones_sb = a("negones_sb", [1, 128], BF16)
        self.invd_sb = a("invd_sb", [128, 128], BF16)
        self.ones16_sb = a("ones16_sb", [16, 128], BF16)
        self.id16_sb = a("id16_sb", [16, 16], F32)
        self.eps_sb = a("eps_sb", [128, 1], F32)
        self.epsden_sb = a("epsden_sb", [128, 1], F32)
        self.mask23_sb = a("mask23_sb", [128, 2, WIN], F32)
        self.rotm_sb = a("rotm_sb", [128, 128], BF16)
        self.id128f_sb = a("id128f_sb", [128, 128], F32)
        self.qtmp_sb = a("qtmp_sb", [128, 4, 512], BF16)
        self.rp_a = a("rp_a", [128, 512], F32)
        self.rp_b = a("rp_b", [128, 512], F32)
        self.mtc_sb = a("mtc_sb", [128, 4], BF16)
        self.mth_sb = a("mth_sb", [128, 4, 2], F32)
        self.nmb_sb = a("nmb_sb", [1, 4], BF16)
        self.gs_sb = a("gs_sb", [128, IC], F32)
        self.us_sb = a("us_sb", [128, IC], F32)
        self.dsraw_sb = a("dsraw_sb", [128, DC], F32)
        self.dsh_sb = a("dsh_sb", [128, DC], F32)
        self.mt_sb = a("mt_sb", [128, 16], F32)
        self.row_sb = a("row_sb", [16, 128], F32)
        self.diag_sb = a("diag_sb", [16, 16], BF16)
        self.negb_sb = a("negb_sb", [128, 16], F32)
        self.scr64a = a("scr64a", [64, 512], F32)
        self.scr64b = a("scr64b", [64, 512], F32)
        self.vfm_sb = a("vfm_sb", [128, 2, 512], BF16)

        self.arena_h = a("arena", [128, ARENA], mybir.dt.uint8)
        self.wb_h = a("wbregion", [128, WB], mybir.dt.uint8)
        self.arena_base = nc.lookup_mloc(self.arena_h).addr
        self.wb_base = nc.lookup_mloc(self.wb_h).addr

        pl = self.place
        ab = self.arena_base
        wb = self.wb_base
        self.ln = pl("ln_sl", [128, DC, TG], BF16, ab + 0)
        self.h = pl("h_sl", [128, IC, TG], BF16, ab + 32768)
        self.xin_fd = pl("xin_fd", [128, 4, 512], F32, ab + 145408)
        self.xout_fd = pl("xout_fd", [128, 4, 512], F32, ab + 153600)
        self.sqbf_l1 = pl("sqbf_l1", [128, DC, 512], BF16, ab + 122880)
        self.sqbf = pl("sqbf_sl", [128, 4, 512], BF16, ab + 139776)
        self.mn_scr = pl("mn_scr", [128, NU, 512], F32, ab + 49152)
        self.var_scr = pl("var_scr", [128, NU, 512], F32, ab + 53248)
        self.sd_scr = pl("sd_scr", [128, NU, 512], F32, ab + 57344)
        self.r_scr = pl("r_scr", [128, NU, 512], BF16, ab + 61440)
        self.mean_sb = pl("mean_sb", [128, NU, 512], BF16, ab + 63488)
        self.d_scr = pl("d_scr", [128, 4, 512], BF16, ab + 65536)
        self.sg_scr = pl("sg_scr", [128, 2, 512], F32, ab + 145408)
        self.ut_scr = pl("ut_scr", [128, 2, 512], F32, ab + 149504)
        self.qrot = pl("qrot_sl", [128, NH, TG], BF16, ab + 32768)
        self.kbuf = pl("kbuf_sl", [128, NH, WIN + TG], BF16, ab + 65536)
        self.vbuf = pl("vbuf_sl", [128, 10, D], BF16, ab + 106496)
        self.tbl_sb = pl("tbl_sl", [128, 4, TG], F32, ab + 147456)
        self.scores = pl("scores_sl", [128, 4, 1024], F32, ab + 0)
        self.psb = pl("p_sl", [128, 4, 1024], BF16, ab + 16384)
        self.rc = pl("rc_sl", [128, 4, WIN], F32, ab + 24576)
        self.rcscr = pl("rcscr_sl", [128, 4, WIN], F32, ab + 28672)
        self.ao_sb = self.qrot
        self.xin_fo = pl("xin_fo", [128, 4, 512], F32, ab + 65536)
        self.xout_fo = pl("xout_fo", [128, 4, 512], F32, ab + 73728)
        self.abuf = pl("abuf", [128, 6, DC, 128], BF16, wb + 0)
        self.dbuf_h = pl("dbuf_h", [128, 3, 22, 128], BF16, ab + 122880)

        ap_ = nc.alloc_psum_tensor
        self.pb = [ap_(f"pb{i}", [128, 512], F32) for i in range(6)]
        self.pb_pl = ap_("pb_pl", [128, 512], F32)
        self.pb_t = ap_("pb_t", [128, 1024], BF16)

        sem = nc.alloc_semaphore
        self.pe = Ctr(sem(name="s_pe"), "pe")
        self.dve = Ctr(sem(name="s_dve"), "dve")
        self.act = Ctr(sem(name="s_act"), "act")
        self.gd = Ctr(sem(name="s_gd"), "gd")
        self.xd = Ctr(sem(name="s_xd"), "xd")
        self.sd = Ctr(sem(name="s_sd"), "sd")
        self.sx = [Ctr(sem(name=f"s_sx{i}"), f"sx{i}") for i in range(2)]
        self.sa = [Ctr(sem(name=f"s_sa{i}"), f"sa{i}") for i in range(6)]
        self.sb = [Ctr(sem(name=f"s_sb{i}"), f"sb{i}") for i in range(3)]
        self.sxi = [Ctr(sem(name=f"s_sxi{i}"), f"sxi{i}") for i in range(4)]
        self.sso = [Ctr(sem(name=f"s_sso{i}"), f"sso{i}") for i in range(4)]
        self.smisc = [Ctr(sem(name=f"s_sm{i}"), f"sm{i}") for i in range(2)]
        self.gp = Ctr(sem(name="s_gp"), "gp")
        self.all_ctrs = ([self.pe, self.dve, self.act, self.gd, self.xd, self.sd,
                          self.gp]
                         + self.sx + self.sa + self.sb + self.sxi + self.sso
                         + self.smisc)
        self.eng_ctrs = {"tensor": (self.pe,), "vector": (self.dve,),
                         "scalar": (self.act,), "sync": (), "gpsimd": (self.gp,)}
        self.sqbf_war = {}
        self.sh_done = {}
        self._shadow_q = []
        self.a_ctr = 0
        self.a_war = {}
        self.b_ctr = 0
        self.b_war = {}
        self.b_extra_w = []
        self.lnrdy = {}
        self.ln_bankfree = {}

    def place(self, name, shape, dtype, addr):
        nc = self.nc
        h = nc.virtual_sbuf_tensor(name, shape, dtype)
        per_part = 1
        for s_ in shape[1:]:
            per_part *= s_
        per_part *= mybir.dt.size(dtype)
        nc._tensor(h.name, list(shape), dtype, type="SB")
        mloc = nc.lookup_mloc(h)
        mloc.allocated = True
        mloc.addr = addr
        return h

    def bar(self):
        self.p.barrier(self.eng_ctrs, self.all_ctrs)

    def soften(self, **eng_waits):
        for e, wl in eng_waits.items():
            self.p.pending[e] = list(wl)
            self.p.seen[e] = {}

    def _sso_totals(self):
        return [(c, c.n) for c in self.sso if c.n > 0]


    def prep(self):
        p = self.p
        loads = [
            (self.ident_sb, self.c_ident), (self.ones_sb, self.c_ones),
            (self.negones_sb, self.c_negones),
            (self.invd_sb, self.c_invd), (self.ones16_sb, self.c_ones16),
            (self.id16_sb, self.c_id16), (self.eps_sb, self.c_eps),
            (self.epsden_sb, self.c_epsden),
            (self.mask23_sb, self.c_mask23), (self.rotm_sb, self.c_rotm),
            (self.id128f_sb, self.c_id128f),
            (self.gs_sb, self.gsd), (self.us_sb, self.usd),
            (self.dsraw_sb, self.dsd),
        ]
        for dst, src in loads:
            p.emit("sync", lambda e, d=dst, s=src: e.dma_start(d[tuple(slice(None) for _ in d.shape)], s[tuple(slice(None) for _ in s.shape)]),
                   inc=self.xd, inc_n=16)
        v_xd = self.xd.n
        p.emit("vector", lambda e: e.tensor_scalar_mul(self.dsh_sb[:, :], self.dsraw_sb[:, :], 0.5),
               waits=[(self.xd, v_xd)], inc=self.dve)
        self.bar()

    def lnin(self, g, src=None):
        p = self.p
        if not hasattr(self, "sqbf_war"):
            self.sqbf_war = {}
        b, toff = g // 2, (g % 2) * TG
        for u in range(NU):
            if src is not None:
                pbm_key, pbq_key = ("pb", 2 * u), ("pb", 2 * u + 1)
                pbm = self.pb[2 * (u % 2)]
                pbq = self.pb[2 * (u % 2) + 1]
                sq_marks = []
                msq_marks = []
                mean_mark = None
                sl = slice(toff + u * 512, toff + u * 512 + 512)
                mld = p.emit("gpsimd", lambda e, sl=sl, u=u: e.dma_start(
                    self.ln[:, :, u * 512:(u + 1) * 512],
                    src[b, :, sl].rearrange("(c p) t -> p c t", p=128)),
                    inc=self.sx[u], inc_n=16)
                if not hasattr(self, "sqbf_l1_war"):
                    self.sqbf_l1_war = {}
                for c in range(DC):
                    sq_w = [(self.sx[u], mld)]
                    if c in self.sqbf_l1_war:
                        sq_w.append(self.sqbf_l1_war[c])
                    m = p.emit("scalar", lambda e, c=c, u=u: e.activation(
                        self.sqbf_l1[:, c, :], self.ln[:, c, u * 512:(u + 1) * 512],
                        AF.Square),
                        waits=sq_w, inc=self.act)
                    sq_marks.append(m)
                for c in range(DC):
                    mean_mark = p.emit("tensor", lambda e, c=c, u=u, pbm=pbm: e.matmul(
                        pbm[:, :], self.invd_sb[:, :],
                        self.ln[:, c, u * 512:(u + 1) * 512],
                        start=(c == 0), stop=(c == DC - 1)),
                        waits=[(self.sx[u], mld)],
                        inc=self.pe)
                for c in range(DC):
                    mm = p.emit("tensor", lambda e, c=c, u=u, pbq=pbq: e.matmul(
                        pbq[:, :], self.invd_sb[:, :], self.sqbf_l1[:, c, :],
                        start=(c == 0), stop=(c == DC - 1)),
                        waits=[(self.act, sq_marks[c])],
                        inc=self.pe)
                    msq_marks.append(mm)
                    self.sqbf_l1_war[c] = (self.pe, mm)
                w_mn = [(self.pe, mean_mark)]
                w_vr = [(self.pe, msq_marks[-1])]
            else:
                pbm_key, pbq_key = (("pb", 3), ("pb", 4)) if u == 0 else (("pb", 5), ("pl",))
                pbm = self.pb[3] if u == 0 else self.pb[5]
                pbq = self.pb[4] if u == 0 else self.pb_pl
                w_mn = []
                w_vr = []
            m_act = p.emit("scalar", lambda e, u=u, pbm=pbm: e.activation(
                self.mn_scr[:, u, :], pbm[:, :], AF.Square),
                waits=w_mn, inc=self.act)
            m_mev = p.emit("scalar", lambda e, u=u, pbm=pbm: e.activation(
                self.mean_sb[:, u, :], pbm[:, :], AF.Copy),
                inc=self.act)
            self.ln_bankfree[pbm_key] = (self.act, m_mev)
            m_dve = p.emit("vector", lambda e, u=u, pbq=pbq: e.tensor_tensor(
                self.var_scr[:, u, :], pbq[:, :], self.mn_scr[:, u, :], ALU.subtract),
                waits=w_vr + [(self.act, m_act)], inc=self.dve)
            self.ln_bankfree[pbq_key] = (self.dve, m_dve)
            m_act = p.emit("scalar", lambda e, u=u: e.activation(
                self.sd_scr[:, u, :], self.var_scr[:, u, :], AF.Ln, bias=self.eps_sb[:, :]),
                waits=[(self.dve, m_dve)], inc=self.act)
            m_act = p.emit("scalar", lambda e, u=u: e.activation(
                self.r_scr[:, u, :], self.sd_scr[:, u, :], AF.Exp, scale=-0.5),
                waits=[(self.act, m_act)], inc=self.act)
            sub_marks = {}
            mul_marks = {}

            def emit_mul(c, u=u):
                w = [(self.dve, sub_marks[c])]
                if c == 0:
                    w.append((self.act, m_act))
                ml = p.emit("vector", lambda e, c=c, u=u: e.tensor_tensor(
                    self.ln[:, c, u * 512:(u + 1) * 512], self.d_scr[:, c % 4, :],
                    self.r_scr[:, u, :], ALU.mult),
                    waits=w, inc=self.dve)
                mul_marks[c] = ml
                self.lnrdy[(u, c)] = (self.dve, ml)

            for c in range(DC):
                w = [(self.act, m_act)] if c == 0 else []
                if c >= 4:
                    w.append((self.dve, mul_marks[c - 4]))
                sub_marks[c] = p.emit("vector", lambda e, c=c, u=u: e.tensor_tensor(
                    self.d_scr[:, c % 4, :], self.ln[:, c, u * 512:(u + 1) * 512],
                    self.mean_sb[:, u, :], ALU.subtract),
                    waits=w, inc=self.dve)
                if c >= 1:
                    emit_mul(c - 1)
            emit_mul(DC - 1)
        self.bar()

    def _aload(self, src_ap, extra_w=()):
        p = self.p
        slot = self.a_ctr % 6
        self.a_ctr += 1
        w = list(extra_w)
        if slot in self.a_war:
            w.append(self.a_war.pop(slot))
        m = p.emit("gpsimd", lambda e, slot=slot, src_ap=src_ap: e.dma_start(
            self.abuf[:, slot, :, :],
            src_ap.rearrange("p (c i) -> p c i", c=DC)),
            waits=w, inc=self.sa[slot], inc_n=16)
        return slot, m

    def fgu(self, ln_entry=None):
        p = self.p
        self.hmul_done = {}
        ev_done = {}
        pend_h = []

        def flush_h(lag=0):
            while len(pend_h) > lag:
                j_, u_, ssl_, ma_, md_ = pend_h.pop(0)
                mh = p.emit("vector", lambda e, j=j_, u=u_, ssl=ssl_: e.tensor_tensor(
                    self.h[:, j, u * 512:(u + 1) * 512], self.sg_scr[:, ssl, :],
                    self.ut_scr[:, ssl, :], ALU.mult),
                    waits=[(self.act, ma_), (self.dve, md_)], inc=self.dve)
                self.hmul_done[ssl_] = mh
        for j in range(IC):
            slg, mg = self._aload(self.gw[j, :, :])
            slu, mu = self._aload(self.uw[j, :, :])
            for u in range(NU):
                par = (j * NU + u) % 2
                pbg = self.pb[par]
                pbu = self.pb[2 + par]
                w_g = [(self.sa[slg], mg)]
                w_u = [(self.sa[slu], mu)]
                if j == 0 and ln_entry is not None:
                    for (bk, wl) in ((("pb", par), w_g), (("pb", 2 + par), w_u)):
                        if bk in self.ln_bankfree:
                            wl.append(self.ln_bankfree[bk])
                        wl.extend(ln_entry)
                key_g = ("g", par)
                key_u = ("u", par)
                if key_g in ev_done:
                    w_g.append((self.act, ev_done[key_g]))
                if key_u in ev_done:
                    w_u.append((self.dve, ev_done[key_u]))
                for c in range(DC):
                    cw = list(w_g) if c == 0 else []
                    if j == 0 and ln_entry is not None:
                        cw.append(self.lnrdy[(u, c)])
                    p.emit("tensor", lambda e, slg=slg, u=u, c=c, pbg=pbg: e.matmul(
                        pbg[:, :], self.abuf[:, slg, c, :],
                        self.ln[:, c, u * 512:(u + 1) * 512],
                        start=(c == 0), stop=(c == DC - 1)),
                        waits=cw,
                        inc=self.pe if c == DC - 1 else None)
                mm_g = self.pe.n
                for c in range(DC):
                    cw = list(w_u) if c == 0 else []
                    if j == 0 and ln_entry is not None:
                        cw.append(self.lnrdy[(u, c)])
                    p.emit("tensor", lambda e, slu=slu, u=u, c=c, pbu=pbu: e.matmul(
                        pbu[:, :], self.abuf[:, slu, c, :],
                        self.ln[:, c, u * 512:(u + 1) * 512],
                        start=(c == 0), stop=(c == DC - 1)),
                        waits=cw,
                        inc=self.pe if c == DC - 1 else None)
                mm_u = self.pe.n
                if u == NU - 1:
                    self.a_war[slg] = (self.pe, mm_g)
                    self.a_war[slu] = (self.pe, mm_u)
                ssl = (j * NU + u) % 2
                sg_w = [(self.pe, mm_g)]
                if ssl in self.hmul_done:
                    sg_w.append((self.dve, self.hmul_done[ssl]))
                ma = p.emit("scalar", lambda e, j=j, ssl=ssl, pbg=pbg: e.activation(
                    self.sg_scr[:, ssl, :], pbg[:, :], AF.Silu, scale=self.gs_sb[:, j:j + 1]),
                    waits=sg_w, inc=self.act)
                ev_done[key_g] = ma
                md_w = [(self.pe, mm_u)]
                if ssl in self.hmul_done:
                    md_w.append((self.dve, self.hmul_done[ssl]))
                md = p.emit("vector", lambda e, j=j, ssl=ssl, pbu=pbu: e.tensor_scalar(
                    self.ut_scr[:, ssl, :], pbu[:, :], self.us_sb[:, j:j + 1], None, ALU.mult),
                    waits=md_w, inc=self.dve)
                ev_done[key_u] = md
                pend_h.append((j, u, ssl, ma, md))
                flush_h(1)
        flush_h(0)
        self.bar()

    def fdn(self, g, xsrc, xdst, shadow=False):
        p = self.p
        if not hasattr(self, "sqbf_war"):
            self.sqbf_war = {}
        b, toff = g // 2, (g % 2) * TG
        ev_done = {}
        stt_done = {}
        st_done = {}
        for j in range(DC):
            hm = []
            for half in range(2):
                slot = self.b_ctr % 3
                self.b_ctr += 1
                w = list(self.b_extra_w)
                self.b_extra_w = []
                if slot in self.b_war:
                    w.append(self.b_war.pop(slot))
                mw = p.emit("gpsimd", lambda e, j=j, slot=slot, half=half: e.dma_start(
                    self.dbuf_h[:, slot, :, :],
                    self.dw[j, :, half * 2816:(half + 1) * 2816]
                        .rearrange("p (c i) -> p c i", c=22)),
                    waits=w, inc=self.sb[slot], inc_n=16)
                hm.append((slot, mw))
            for u in range(NU):
                r = (j * NU + u) % 4
                sl = slice(toff + u * 512, toff + u * 512 + 512)
                ld_w = []
                if r in stt_done:
                    ld_w.append((self.dve, stt_done[r]))
                mx = p.emit("sync", lambda e, j=j, r=r, sl=sl: e.dma_start(
                    self.xin_fd[:, r, :], xsrc[b, j * 128:(j + 1) * 128, sl]),
                    waits=ld_w, inc=self.sxi[r], inc_n=16)
                par = (j * NU + u) % 3
                pbd = self.pb[par]
                w0 = [(self.sb[hm[0][0]], hm[0][1])]
                if par in ev_done:
                    w0.append((self.dve, ev_done[par]))
                m_halfa = None
                for c in range(IC):
                    half = 0 if c < 22 else 1
                    slot, mw = hm[half]
                    cw = w0 if c == 0 else (
                        [(self.sb[slot], mw)] if c == 22 else ())
                    want_inc = (c == IC - 1) or (c == 21 and u == NU - 1)
                    mk = p.emit("tensor", lambda e, slot=slot, u=u, c=c, half=half, pbd=pbd: e.matmul(
                        pbd[:, :], self.dbuf_h[:, slot, c - half * 22, :],
                        self.h[:, c, u * 512:(u + 1) * 512],
                        start=(c == 0), stop=(c == IC - 1)),
                        waits=cw,
                        inc=self.pe if want_inc else None)
                    if c == 21 and u == NU - 1:
                        m_halfa = mk
                mm = self.pe.n
                if u == NU - 1:
                    self.b_war[hm[0][0]] = (self.pe, m_halfa)
                    self.b_war[hm[1][0]] = (self.pe, mm)
                stt_w = [(self.pe, mm), (self.sxi[r], mx)]
                if r in st_done:
                    stt_w.append((self.sso[r], st_done[r]))
                if r in self.sh_done:
                    stt_w.append(self.sh_done.pop(r))
                me = p.emit("vector", lambda e, j=j, r=r, pbd=pbd: e.scalar_tensor_tensor(
                    self.xout_fd[:, r, :], pbd[:, :], self.dsh_sb[:, j:j + 1],
                    self.xin_fd[:, r, :], ALU.mult, ALU.add),
                    waits=stt_w, inc=self.dve)
                ev_done[par] = me
                stt_done[r] = me
                st_done[r] = p.emit("scalar", lambda e, j=j, r=r, sl=sl: e.dma_start(
                    xdst[b, j * 128:(j + 1) * 128, sl], self.xout_fd[:, r, :]),
                    waits=[(self.dve, me)], inc=self.sso[r], inc_n=16)
                if shadow:
                    self.sh_done[r] = self._shadow_unit(j, u, r, self.xout_fd, me)
                    self._shadow_flush(2)
        self._shadow_flush(0)
        self.bar()

    def _shadow_unit(self, j, u, r, xout, me):
        p = self.p
        slot = (j * NU + u) % 4
        msh = p.emit("scalar", lambda e, j=j, u=u, r=r: e.activation(
            self.ln[:, j, u * 512:(u + 1) * 512], xout[:, r, :], AF.Copy),
            waits=[(self.dve, me)], inc=self.act)
        sq_w = [(self.act, msh)]
        if slot in self.sqbf_war:
            sq_w.append(self.sqbf_war[slot])
        for c_ in self.smisc:
            if c_.n > 0:
                sq_w.append((c_, c_.n))
        msq_a = p.emit("scalar", lambda e, j=j, u=u, slot=slot: e.activation(
            self.sqbf[:, slot, :], self.ln[:, j, u * 512:(u + 1) * 512], AF.Square),
            waits=sq_w, inc=self.act)
        self._shadow_q.append((j, u, slot, msh, msq_a))
        return (self.act, msh)

    def _shadow_mms(self, j, u, slot, msh, msq_a):
        p = self.p
        pbm = self.pb[3] if u == 0 else self.pb[5]
        pbq = self.pb[4] if u == 0 else self.pb_pl
        p.emit("tensor", lambda e, j=j, u=u, pbm=pbm: e.matmul(
            pbm[:, :], self.invd_sb[:, :], self.ln[:, j, u * 512:(u + 1) * 512],
            start=(j == 0), stop=(j == DC - 1), skip_group_check=True),
            waits=[(self.act, msh)], inc=self.pe)
        mq = p.emit("tensor", lambda e, j=j, slot=slot, pbq=pbq: e.matmul(
            pbq[:, :], self.invd_sb[:, :], self.sqbf[:, slot, :],
            start=(j == 0), stop=(j == DC - 1), skip_group_check=True),
            waits=[(self.act, msq_a)], inc=self.pe)
        self.sqbf_war[slot] = (self.pe, mq)

    def _shadow_flush(self, lag):
        while len(self._shadow_q) > lag:
            self._shadow_mms(*self._shadow_q.pop(0))


    def qkv(self, g, ln_entry=None):
        self.p.cur_label = "qkv"
        p = self.p
        self._qkv_ln_entry = ln_entry
        b, toff = g // 2, (g % 2) * TG
        batch_start = (g % 2 == 0)
        mtbl = p.emit("sync", lambda e: e.dma_start(
            self.tbl_sb[:, :, :], self.c_tbl[:, :, toff:toff + TG]),
            inc=self.xd, inc_n=16)
        if not batch_start:
            p.emit("sync", lambda e: e.dma_start(
                self.kbuf[:, :, 0:WIN], self.kprev_d[:, :, :]),
                waits=[(self.smisc[0], self.smisc[0].n), (self.smisc[1], self.smisc[1].n),
                       (self.pe, self.pe.n), (self.dve, self.dve.n), (self.act, self.act.n)],
                inc=self.xd, inc_n=16)
            p.emit("sync", lambda e: e.dma_start(
                self.vbuf[:, 0:2, :], self.vprev_d[:, :, :]),
                inc=self.xd, inc_n=16)

        slabs = ([("q", j) for j in range(DC)] + [("k", j) for j in range(DC)]
                 + [("v", j) for j in range(DC)])
        bank_free = {}
        rot_free = {}
        rp_free = [None]
        qtmp_free = {}
        vfm_free = {}
        pbt_free = {}
        srcs = {"q": self.wq, "k": self.wk, "v": self.wv}
        rotbanks = [self.pb[4], self.pb[5], self.pb_pl]
        deferred = []
        unit = 0

        def qk_bundle(kind, j, u, tslot, rslot, mc):
            def go():
                ti = 2 if kind == "q" else 0
                pbr = rotbanks[rslot]
                w_r = [(self.act, mc)]
                if rslot in rot_free:
                    w_r.append(rot_free[rslot])
                elif self._qkv_ln_entry is not None:
                    bk = [("pb", 4), ("pb", 5), ("pl",)][rslot]
                    if bk in self.ln_bankfree:
                        w_r.append(self.ln_bankfree[bk])
                    w_r.extend(self._qkv_ln_entry)
                mr = p.emit("tensor", lambda e, tslot=tslot, pbr=pbr: e.matmul(
                    pbr[:, :], self.rotm_sb[:, :], self.qtmp_sb[:, tslot, :],
                    start=True, stop=True),
                    waits=w_r, inc=self.pe)
                w_a1 = [(self.act, mc), (self.xd, mtbl)]
                if rp_free[0] is not None:
                    w_a1.append(rp_free[0])
                ma1 = p.emit("vector", lambda e, tslot=tslot, ti=ti, u=u: e.tensor_mul(
                    self.rp_a[:, :], self.qtmp_sb[:, tslot, :],
                    self.tbl_sb[:, ti, u * 512:(u + 1) * 512]),
                    waits=w_a1, inc=self.dve)
                ma2 = p.emit("vector", lambda e, pbr=pbr, ti=ti, u=u: e.tensor_mul(
                    self.rp_b[:, :], pbr[:, :],
                    self.tbl_sb[:, ti + 1, u * 512:(u + 1) * 512]),
                    waits=[(self.pe, mr)], inc=self.dve)
                rot_free[rslot] = (self.dve, ma2)
                if kind == "q":
                    dst = self.qrot[:, j, u * 512:(u + 1) * 512]
                else:
                    dst = self.kbuf[:, j, WIN + u * 512: WIN + (u + 1) * 512]
                madd = p.emit("vector", lambda e, dst=dst: e.tensor_add(
                    dst, self.rp_a[:, :], self.rp_b[:, :]),
                    waits=[(self.dve, ma2)], inc=self.dve)
                rp_free[0] = (self.dve, madd)
                qtmp_free[tslot] = [(self.dve, ma1), (self.pe, mr)]
            return go

        def v_bundle(j, u, vslot, ph, mc):
            def go():
                w_t = [(self.act, mc)]
                if ph in pbt_free:
                    w_t.append(pbt_free[ph])
                mtt = None
                for tc in range(4):
                    mtt = p.emit("tensor", lambda e, ph=ph, tc=tc, vslot=vslot: e.transpose(
                        self.pb_t[:, ph * 512 + tc * 128: ph * 512 + (tc + 1) * 128],
                        self.vfm_sb[:, vslot, tc * 128:(tc + 1) * 128],
                        self.ident_sb[:, :]),
                        waits=w_t if tc == 0 else (),
                        inc=self.pe)
                vfm_free[vslot] = (self.pe, mtt)
                mdv = p.emit("vector", lambda e, ph=ph, u=u, j=j: e.tensor_copy(
                    self.vbuf[:, 2 + u * 4: 2 + u * 4 + 4, j * 128:(j + 1) * 128],
                    bass.AP(self.pb_t, ph * 512, [[1024, 128], [128, 4], [1, 128]])),
                    waits=[(self.pe, mtt)], inc=self.dve)
                pbt_free[ph] = (self.dve, mdv)
            return go

        banks_seen = set()
        for s_idx, (kind, j) in enumerate(slabs):
            slot, msl = self._aload(srcs[kind][j, :, :])
            for u in range(NU):
                while deferred and deferred[0][0] <= unit - 2:
                    deferred.pop(0)[1]()
                bank = (2 * s_idx + u) % 4
                pb = self.pb[bank]
                w0 = [(self.sa[slot], msl)]
                if bank in bank_free:
                    w0.append(bank_free[bank])
                elif ln_entry is not None and bank not in banks_seen:
                    bk = ("pb", bank)
                    if bk in self.ln_bankfree:
                        w0.append(self.ln_bankfree[bk])
                    w0.extend(ln_entry)
                banks_seen.add(bank)
                for c in range(DC):
                    cw = list(w0) if c == 0 else []
                    if s_idx == 0 and ln_entry is not None:
                        cw.append(self.lnrdy[(u, c)])
                    mmark = p.emit("tensor", lambda e, slot=slot, c=c, u=u, pb=pb: e.matmul(
                        pb[:, :], self.abuf[:, slot, c, :],
                        self.ln[:, c, u * 512:(u + 1) * 512],
                        start=(c == 0), stop=(c == DC - 1)),
                        waits=cw,
                        inc=self.pe)
                if u == NU - 1:
                    self.a_war[slot] = (self.pe, mmark)
                if kind in ("q", "k"):
                    tslot = unit % 4
                    w_c = [(self.pe, mmark)]
                    if tslot in qtmp_free:
                        w_c.extend(qtmp_free[tslot])
                    mc = p.emit("scalar", lambda e, tslot=tslot, pb=pb: e.activation(
                        self.qtmp_sb[:, tslot, :], pb[:, :], AF.Copy),
                        waits=w_c, inc=self.act)
                    bank_free[bank] = (self.act, mc)
                    deferred.append((unit, qk_bundle(kind, j, u, tslot, unit % 3, mc)))
                else:
                    vslot = unit % 2
                    w_c = [(self.pe, mmark)]
                    if vslot in vfm_free:
                        w_c.append(vfm_free[vslot])
                    mc = p.emit("scalar", lambda e, vslot=vslot, pb=pb: e.activation(
                        self.vfm_sb[:, vslot, :], pb[:, :], AF.Copy),
                        waits=w_c, inc=self.act)
                    bank_free[bank] = (self.act, mc)
                    deferred.append((unit, v_bundle(j, u, vslot, unit % 2, mc)))
                unit += 1
        while deferred:
            deferred.pop(0)[1]()
        self.bar()

    def core(self, g):
        self.p.cur_label = "core"
        p = self.p
        NWH = 4 * NH
        marks = {}

        def geom(widx):
            w, h = widx // NH, widx % NH
            wa = (g % 2) * 4 + w
            return w, h, wa > 0

        def emit_exp(widx):
            _, _, has_prev = geom(widx)
            nch = 4 if has_prev else 2
            sl = widx % 4
            w_e = [marks[("negb", widx)]] + list(marks.get(("psb_slot", sl), []))
            if nch == 4:
                w_e.append(marks[("ev_prev", widx)])
            mex = p.emit("scalar", lambda e, sl=sl, nch=nch: e.activation(
                self.psb[:, sl, 0:nch * 256], self.scores[:, sl, 0:nch * 256],
                AF.Exp, bias=self.negb_sb[:, sl:sl + 1]),
                waits=w_e, inc=self.act)
            marks[("scores_slot", sl)] = [(self.act, mex)]
            marks[("negb_slot", sl)] = [(self.act, mex)]
            marks[("exp", widx)] = (self.act, mex)

        def emit_scores(widx):
            w, h, has_prev = geom(widx)
            nch = 4 if has_prev else 2
            npair = nch // 2
            sl = widx % 4
            bset = 4 if (widx % 2 == 0) else 2
            for pr in range(npair):
                pbs = self.pb[bset + pr]
                w0 = list(marks.get(("scfree", bset + pr), []))
                for kk in range(2):
                    kc = pr * 2 + kk
                    kcol = w * 256 + (0 if has_prev else 256) + kc * 128
                    mmark = p.emit("tensor", lambda e, pbs=pbs, kk=kk, h=h, kcol=kcol, w=w: e.matmul(
                        pbs[:, kk * 256:(kk + 1) * 256],
                        self.kbuf[:, h, kcol:kcol + 128],
                        self.qrot[:, h, w * 256:(w + 1) * 256],
                        start=(kk == 0), stop=(kk == 1), skip_group_check=True),
                        waits=w0 if kk == 0 else (),
                        inc=self.pe)
                marks[("scmm", widx, pr)] = (self.pe, mmark)

        def emit_gmax(widx):
            _, _, has_prev = geom(widx)
            npair = 2 if has_prev else 1
            sl = widx % 4
            w_n = [marks[("evac_dve", widx)]] + list(marks.get(("nmb_slot", sl), []))
            mnm = p.emit("gpsimd", lambda e, sl=sl, npair=npair: e.tensor_reduce(
                self.nmb_sb[0:1, sl:sl + 1],
                self.scores[:, sl, (npair - 1) * 512:npair * 512],
                AX.XYZWC, ALU.max),
                waits=w_n, inc=self.gp)
            marks[("nm", widx)] = (self.gp, mnm)

        def emit_bcast(widx):
            sl = widx % 4
            w_b = [marks[("nm", widx)]] + list(marks.get(("pbpl", sl), []))
            mbc = p.emit("tensor", lambda e, sl=sl: e.matmul(
                self.pb_pl[:, sl:sl + 1], self.negones_sb[0:1, :],
                self.nmb_sb[0:1, sl:sl + 1],
                start=True, stop=True, skip_group_check=True),
                waits=w_b, inc=self.pe)
            marks[("nmb_slot", sl)] = [(self.pe, mbc)]
            marks[("bc", widx)] = (self.pe, mbc)

        def emit_negb(widx):
            sl = widx % 4
            mng = p.emit("scalar", lambda e, sl=sl: e.activation(
                self.negb_sb[:, sl:sl + 1], self.pb_pl[:, sl:sl + 1], AF.Copy),
                waits=[marks[("bc", widx)]] + list(marks.get(("negb_slot", sl), [])),
                inc=self.act)
            marks[("pbpl", sl)] = [(self.act, mng)]
            marks[("negb", widx)] = (self.act, mng)

        def emit_evac(widx):
            _, _, has_prev = geom(widx)
            nch = 4 if has_prev else 2
            npair = nch // 2
            sl = widx % 4
            bset = 4 if (widx % 2 == 0) else 2
            for pr in range(npair):
                pbs = self.pb[bset + pr]
                wv = [marks[("scmm", widx, pr)]] + list(marks.get(("scores_slot", sl), []))
                if pr == npair - 1:
                    me = p.emit("vector", lambda e, pbs=pbs, sl=sl, pr=pr: e.tensor_add(
                        self.scores[:, sl, pr * 512:(pr + 1) * 512], pbs[:, :],
                        bass.AP(self.mask23_sb, 0, [[512, 128], [1, 512]])),
                        waits=wv, inc=self.dve)
                    marks[("scfree", bset + pr)] = [(self.dve, me)]
                    marks[("evac_dve", widx)] = (self.dve, me)
                else:
                    me = p.emit("vector", lambda e, pbs=pbs, sl=sl, pr=pr: e.tensor_copy(
                        self.scores[:, sl, pr * 512:(pr + 1) * 512], pbs[:, :]),
                        waits=wv, inc=self.dve)
                    marks[("scfree", bset + pr)] = [(self.dve, me)]
                    marks[("ev_prev", widx)] = (self.dve, me)

        def emit_denpv(widx):
            w, h, has_prev = geom(widx)
            nch = 4 if has_prev else 2
            sl = widx % 4
            hb = widx % 2
            pbb = self.pb[hb]
            w_d = [marks[("exp", widx)]] + list(marks.get(("denout", hb), []))
            for kc in range(nch):
                p.emit("tensor", lambda e, pbb=pbb, sl=sl, kc=kc, nch=nch: e.matmul(
                    pbb[:, 0:256], self.ones_sb[:, :],
                    self.psb[:, sl, kc * 256:(kc + 1) * 256],
                    start=(kc == 0), stop=(kc == nch - 1), skip_group_check=True),
                    waits=w_d if kc == 0 else (),
                    inc=self.pe)
            for kc in range(nch):
                vidx = (2 * w + kc) if has_prev else (2 * w + 2 + kc)
                mpv = p.emit("tensor", lambda e, pbb=pbb, sl=sl, kc=kc, vidx=vidx, h=h, nch=nch: e.matmul(
                    pbb[:, 256:512], self.vbuf[:, vidx, h * 128:(h + 1) * 128],
                    self.psb[:, sl, kc * 256:(kc + 1) * 256],
                    start=False, stop=(kc == nch - 1), skip_group_check=True),
                    inc=self.pe)
            marks[("psb_slot", sl)] = [(self.pe, mpv)]
            marks[("denpv", widx)] = (self.pe, mpv)

        def emit_rcscr(widx):
            sl = widx % 4
            hb = widx % 2
            pbb = self.pb[hb]
            w_r = [marks[("denpv", widx)]] + list(marks.get(("rcln_slot", sl), []))
            mr1 = p.emit("scalar", lambda e, pbb=pbb, sl=sl: e.activation(
                self.rcscr[:, sl, :], pbb[:, 0:256], AF.Ln, bias=self.epsden_sb[:, :]),
                waits=w_r, inc=self.act)
            marks[("rcscr", widx)] = (self.act, mr1)

        def emit_rc_ao(widx):
            w, h, has_prev = geom(widx)
            npair = 2 if has_prev else 1
            sl = widx % 4
            hb = widx % 2
            pbb = self.pb[hb]
            w_r2 = [marks[("rcscr", widx)]] + list(marks.get(("rc_slot", sl), []))
            mr2 = p.emit("scalar", lambda e, sl=sl: e.activation(
                self.rc[:, sl, :], self.rcscr[:, sl, :], AF.Exp, scale=-1.0),
                waits=w_r2, inc=self.act)
            marks[("rcln_slot", sl)] = [(self.act, mr2)]
            w_a = [(self.act, mr2), marks[("scmm", widx, npair - 1)]]
            mao = p.emit("vector", lambda e, h=h, w=w, sl=sl, pbb=pbb: e.tensor_mul(
                self.qrot[:, h, w * 256:(w + 1) * 256], pbb[:, 256:512],
                self.rc[:, sl, :]),
                waits=w_a, inc=self.dve)
            marks[("rc_slot", sl)] = [(self.dve, mao)]
            marks[("denout", hb)] = [(self.act, marks[("rcscr", widx)][1]), (self.dve, mao)]
            marks[("ao", widx)] = (self.dve, mao)

        for t in range(NWH + 5):
            if 0 <= t - 2 < NWH:
                emit_exp(t - 2)
            if t < NWH:
                emit_scores(t)
            if 0 <= t - 1 < NWH:
                emit_bcast(t - 1)
            if 0 <= t - 4 < NWH:
                emit_rc_ao(t - 4)
            if 0 <= t - 1 < NWH:
                emit_negb(t - 1)
            if t < NWH:
                emit_evac(t)
                emit_gmax(t)
            if 0 <= t - 3 < NWH:
                emit_denpv(t - 3)
                emit_rcscr(t - 3)
        self.bar()

    def oproj_preload(self, qkv_end):
        self.wo_pre = {}
        for j in range(6):
            self.wo_pre[j] = self._aload(self.wo[j, :, :],
                                         extra_w=[(self.pe, qkv_end)])

    def oproj(self, g, xsrc, xdst, shadow=False):
        p = self.p
        b, toff = g // 2, (g % 2) * TG
        if g % 2 == 0:
            p.emit("sync", lambda e: e.dma_start(
                self.kprev_d[:, :, :], self.kbuf[:, :, TG:TG + WIN]),
                inc=self.smisc[0], inc_n=16)
            p.emit("sync", lambda e: e.dma_start(
                self.vprev_d[:, :, :], self.vbuf[:, 8:10, :]),
                inc=self.smisc[1], inc_n=16)
        bank_free = {}
        xin_done = {}
        st_done = {}
        for j in range(DC):
            if hasattr(self, "wo_pre") and j in self.wo_pre:
                slot, msl = self.wo_pre.pop(j)
            else:
                slot, msl = self._aload(self.wo[j, :, :])
            for u in range(NU):
                r = (j * NU + u) % 4
                sl = slice(toff + u * 512, toff + u * 512 + 512)
                ld_w = []
                if r in xin_done:
                    ld_w.append((self.dve, xin_done[r]))
                mx = p.emit("sync", lambda e, j=j, r=r, sl=sl: e.dma_start(
                    self.xin_fo[:, r, :], xsrc[b, j * 128:(j + 1) * 128, sl]),
                    waits=ld_w, inc=self.sxi[r], inc_n=16)
                bank = (j * NU + u) % (3 if shadow else 4)
                pb = self.pb[bank]
                w0 = [(self.sa[slot], msl)]
                if bank in bank_free:
                    w0.append(bank_free[bank])
                for c in range(DC):
                    mmark = p.emit("tensor", lambda e, slot=slot, c=c, u=u, pb=pb: e.matmul(
                        pb[:, :], self.abuf[:, slot, c, :],
                        self.ao_sb[:, c, u * 512:(u + 1) * 512],
                        start=(c == 0), stop=(c == DC - 1)),
                        waits=w0 if c == 0 else (),
                        inc=self.pe)
                if u == NU - 1:
                    self.a_war[slot] = (self.pe, mmark)
                stt_w = [(self.pe, mmark), (self.sxi[r], mx)]
                if r in st_done:
                    stt_w.append((self.sso[r], st_done[r]))
                if r in self.sh_done:
                    stt_w.append(self.sh_done.pop(r))
                me = p.emit("vector", lambda e, r=r, pb=pb: e.tensor_add(
                    self.xout_fo[:, r, :], pb[:, :], self.xin_fo[:, r, :]),
                    waits=stt_w, inc=self.dve)
                bank_free[bank] = (self.dve, me)
                xin_done[r] = me
                st_done[r] = p.emit("scalar", lambda e, j=j, r=r, sl=sl: e.dma_start(
                    xdst[b, j * 128:(j + 1) * 128, sl], self.xout_fo[:, r, :]),
                    waits=[(self.dve, me)], inc=self.sso[r], inc_n=16)
                if shadow:
                    self.sh_done[r] = self._shadow_unit(j, u, r, self.xout_fo, me)
                    self._shadow_flush(2)
        self._shadow_flush(0)
        self.bar()


    def build(self):
        self.prep()
        pe = self.pe
        prev_dn_end = 0
        g2_end_prev = 0
        for g in range(NG):
            self.soften(gpsimd=[(pe, g2_end_prev)])
            self.lnin(g, self.xin)
            l1_end = pe.n
            self.soften(gpsimd=[], tensor=[])
            self.fgu(ln_entry=[])
            fgu_end = pe.n
            if self.stage == 1:
                self.b_extra_w = [(pe, l1_end)]
                self.fdn(g, self.xin, self.y)
                continue
            self.soften(sync=[(pe, fgu_end), (self.dve, self.dve.n),
                              (self.act, self.act.n)])
            self.b_extra_w = [(pe, l1_end)]
            self.fdn(g, self.xin, self.x1, shadow=True)
            dn1_end = pe.n
            dn1_dve = self.dve.n
            self.lnin(g)
            self.soften(gpsimd=[], tensor=[],
                        sync=[(pe, dn1_end), (self.dve, self.dve.n)] + self._sso_totals())
            self.qkv(g, ln_entry=[(self.dve, dn1_dve)])
            qkv_end = pe.n
            self.oproj_preload(qkv_end)
            self.soften(sync=[])
            self.core(g)
            core_end = pe.n
            self.soften(gpsimd=[(pe, qkv_end)],
                        sync=[(pe, core_end)] + self._sso_totals())
            if self.stage == 2:
                self.oproj(g, self.x1, self.y)
                continue
            self.oproj(g, self.x1, self.x2, shadow=True)
            op_end = pe.n
            op_dve = self.dve.n
            self.lnin(g)
            self.soften(gpsimd=[], tensor=[])
            self.fgu(ln_entry=[(self.dve, op_dve)])
            g2_end_prev = pe.n
            self.soften(sync=[(pe, pe.n), (self.dve, self.dve.n),
                              (self.act, self.act.n)] + self._sso_totals())
            self.b_extra_w = ([(pe, op_end)]
                              + [(c, c.n) for c in self.smisc if c.n > 0])
            self.fdn(g, self.x2, self.y)
            prev_dn_end = pe.n
        fin = [(c, c.n) for c in self.sso if c.n > 0]
        if self.sd.n > 0:
            fin.append((self.sd, self.sd.n))
        self.p.emit("scalar", None, waits=fin)
        build_block(self.nc, self.p)



def _prep_weights(inputs):
    gw = inputs["L0_ffn_gate_w"]
    uw = inputs["L0_ffn_up_w"]
    dw = inputs["L0_ffn_down_w"]
    arr = {}
    arr["gw"] = np.ascontiguousarray(gw.reshape(IC, 128, DC, 128).transpose(0, 3, 2, 1).reshape(IC, 128, DC * 128))
    arr["uw"] = np.ascontiguousarray(uw.reshape(IC, 128, DC, 128).transpose(0, 3, 2, 1).reshape(IC, 128, DC * 128))
    arr["dw"] = np.ascontiguousarray(dw.reshape(DC, 128, IC, 128).transpose(0, 3, 2, 1).reshape(DC, 128, IC * 128))
    arr["gsd"] = np.ascontiguousarray(inputs["L0_ffn_gate_s"].reshape(IC, 128).T)
    arr["usd"] = np.ascontiguousarray(inputs["L0_ffn_up_s"].reshape(IC, 128).T)
    arr["dsd"] = np.ascontiguousarray(inputs["L0_ffn_down_s"].reshape(DC, 128).T)
    for nm, w in [("wqt", inputs["L0_attn_Wq"]), ("wkt", inputs["L0_attn_Wk"]),
                  ("wvt", inputs["L0_attn_Wv"]), ("wot", inputs["L0_attn_Wo"])]:
        arr[nm] = np.ascontiguousarray(
            np.asarray(w).reshape(DC, 128, DC, 128).transpose(2, 1, 0, 3).reshape(DC, 128, DC * 128))
    return arr


def kernel(**inputs):
    hs = np.asarray(inputs["L0_attn_hidden_states"])
    warr = _prep_weights({k: np.asarray(v) for k, v in inputs.items()})

    nc = bass.Bass("TRN2", target_bir_lowering=False)
    mk = MK(nc)
    mk.build()

    in_maps = []
    for core in range(N_CORES):
        xin_c = np.ascontiguousarray(hs[2 * core:2 * core + 2].transpose(0, 2, 1))
        m = {"xin": xin_c}
        m.update(warr)
        in_maps.append(m)

    res = run_bass_kernel_spmd(nc, in_maps, core_ids=list(range(N_CORES)))
    out = np.empty((16, S, D), dtype=np.float32)
    for core in range(N_CORES):
        out[2 * core:2 * core + 2] = res.results[core]["y"].transpose(0, 2, 1)
    return out


if __name__ == "__main__":
    pass
